# version 10
# baseline (speedup 1.0000x reference)
import numpy as np
from contextlib import ExitStack
import concourse.bass as bass
import concourse.mybir as mybir
from concourse.bass_utils import run_bass_kernel_spmd

F32 = mybir.dt.float32
BF16 = mybir.dt.bfloat16
AF = mybir.ActivationFunctionType
ALU = mybir.AluOpType
AX = mybir.AxisListType

NCORES = 8
T = 4096
D = 1024
DC = D // 128
NT = T // 128
GAMMA = 0.96875
LN_EPS = 1e-5
WIN = 26
I0 = 768
E0 = 16
H1 = 4096
E1 = 8


class Sem:
    def __init__(self, h):
        self.h = h
        self.v = 0

    def inc(self, n=1):
        self.v += n
        return self.v


class Phase:
    ENGS = ("tensor", "vector", "scalar", "sync", "gpsimd")

    def __init__(self, nc, es, tag="", sem_es=None):
        self.nc = nc
        self.es = es
        self.sem_es = sem_es if sem_es is not None else es
        self.tag = tag
        self.streams = {k: [] for k in self.ENGS}
        self._semid = 0

    def sem(self, name):
        self._semid += 1
        return Sem(self.sem_es.enter_context(self.nc.semaphore(f"{self.tag}{name}{self._semid}")))

    def esem(self, eng):
        if not hasattr(self, "_esems"):
            self._esems = {}
        if eng not in self._esems:
            self._esems[eng] = self.sem(f"E{eng[:2]}")
        return self._esems[eng]

    def sb(self, name, shape, dt=F32):
        return self.es.enter_context(self.nc.sbuf_tensor(f"{self.tag}{name}", shape, dt))

    def ps(self, name, shape, dt=F32):
        return self.es.enter_context(self.nc.psum_tensor(f"{self.tag}{name}", shape, dt))

    def on(self, eng, fn):
        self.streams[eng].append(fn)

    def wait(self, eng, sem, tgt):
        if tgt is None or tgt <= 0:
            return
        self.streams[eng].append(lambda e, s=sem.h, t=tgt: e.wait_ge(s, t))

    def dma(self, q, out, in_, sem):
        tgt = sem.inc(16)
        self.streams[q].append(
            lambda e, o=out, i=in_, s=sem.h: e.dma_start(out=o, in_=i).then_inc(s, 16))
        return tgt

    def op(self, eng, fn, sem=None):
        if eng in ("vector", "scalar", "gpsimd"):
            cs = self.esem(eng)
            assert sem is None or sem is cs, f"{eng} ops must use esem"
            if cs.v > 0:
                self.streams[eng].append(lambda e, s=cs.h, t=cs.v: e.wait_ge(s, t))
            tgt = cs.inc(1)
            self.streams[eng].append(lambda e, f=fn, s=cs.h: f(e).then_inc(s, 1))
            return tgt
        if sem is None:
            self.streams[eng].append(lambda e, f=fn: f(e))
            return None
        tgt = sem.inc(1)
        self.streams[eng].append(lambda e, f=fn, s=sem.h: f(e).then_inc(s, 1))
        return tgt

    def emit(self):
        with self.nc.Block(self.tag or None) as blk:
            for name in self.ENGS:
                fns = self.streams[name]
                if not fns:
                    continue

                def body(eng, fns=fns):
                    for f in fns:
                        f(eng)
                getattr(blk, name)(body)



def emit_prep(ph, transpose_jobs, cast_jobs, small_jobs, src_bf16=False):
    nc = ph.nc
    castbuf = [ph.sb(f"pcast{i}", [128, 8 * 1024], BF16) for i in range(2)]
    stage = [ph.sb(f"pstage{i}", [128, 8 * 1024], BF16) for i in range(2)]
    castf32 = None if src_bf16 else [ph.sb(f"pcastf{i}", [128, 8 * 1024], F32) for i in range(2)]
    buf16 = ph.sb("pbuf16", [16, 1024], BF16)
    identb = ph.sb("pident", [128, 128], BF16)
    onesf = ph.sb("pones", [128, 128], F32)
    pb = [ph.ps(f"ppsum{i}", [128, 1024], F32) for i in range(2)]

    ld = [ph.sem("ld0"), ph.sem("ld1"), ph.sem("lds")]
    lcf = [ph.sem("lcf0"), ph.sem("lcf1")]
    tr = ph.sem("tr")
    ev = ph.esem("vector")
    sa = ph.esem("scalar")
    stg = [ph.sem("stg0"), ph.sem("stg1")]
    stc = [ph.sem("stc0"), ph.sem("stc1")]
    pre = ph.esem("gpsimd")

    t_on = ph.op("gpsimd", lambda e: e.memset(onesf[:], 1.0), pre)
    ph.wait("gpsimd", pre, t_on)
    t_pre = ph.op("gpsimd", lambda e: e.affine_select(
        identb[:], onesf[:], [[-1, 128]], ALU.is_equal, 0.0, channel_multiplier=1), pre)
    ph.wait("tensor", pre, t_pre)

    gi = 0
    castbuf_guard = {}
    stage_guard = {}
    bank_guard = {}
    bank_i = 0

    castf32_guard = {}

    def wait2(eng, guard):
        if guard is not None:
            ph.wait(eng, guard[0], guard[1])

    def do_group(src_grp, nstrips, CB, dst_store_out, is_small=False, rsmall=0):
        C = CB * 128
        """One transpose group: <=8 strips of 128 rows (or one small r-row strip).
        src_grp: DRAM AP arranged to match the cast-DMA target layout.
        dst_store_out: DRAM AP for the single store, matching the stage view.
        """
        nonlocal gi, bank_i
        par = gi % 2
        q = (rsmall if is_small else nstrips * 128)
        if is_small:
            wait2("gpsimd", castbuf_guard.get("small"))
            t_ld = ph.dma("gpsimd", buf16[0:rsmall, 0:CB * 128], src_grp, ld[2])
            ph.wait("tensor", ld[2], t_ld)
        elif gi % 2 == 0:
            wait2("gpsimd", castbuf_guard.get(par))
            view = castbuf[par][:, 0:nstrips * C].rearrange("p (s c) -> p s c", s=nstrips)
            t_ld = ph.dma("gpsimd", view, src_grp, ld[par])
            ph.wait("tensor", ld[par], t_ld)
        elif src_bf16:
            wait2("sync", castbuf_guard.get(par))
            view = castbuf[par][:, 0:nstrips * C].rearrange("p (s c) -> p s c", s=nstrips)
            t_ld = ph.dma("sync", view, src_grp, lcf[par])
            ph.wait("tensor", lcf[par], t_ld)
        else:
            if castf32_guard.get(par) is not None:
                ph.wait("sync", sa, castf32_guard[par])
            viewf = castf32[par][:, 0:nstrips * C].rearrange("p (s c) -> p s c", s=nstrips)
            t_lf = ph.dma("sync", viewf, src_grp, lcf[par])
            wait2("scalar", castbuf_guard.get(par))
            ph.wait("scalar", lcf[par], t_lf)
            t_cast = ph.op("scalar", lambda e, par=par, n=nstrips * C: e.activation(
                castbuf[par][:, 0:n], castf32[par][:, 0:n], AF.Copy), sa)
            castf32_guard[par] = t_cast
            ph.wait("tensor", sa, t_cast)
        if stage_guard.get(par) is not None:
            ph.wait("vector", stg[par], stage_guard[par])
        ev_end = None
        for c in range(CB):
            bpar = bank_i % 2
            bank_i += 1
            ph.wait("tensor", ev, bank_guard.get(bpar))
            for s in range(nstrips):
                if is_small:
                    fn = lambda e, c=c, s=s, bpar=bpar, r=rsmall: e.matmul(
                        pb[bpar][:, s * r:(s + 1) * r],
                        buf16[0:r, c * 128:(c + 1) * 128], identb[0:r, 0:r],
                        start=True, stop=True)
                else:
                    fn = lambda e, c=c, s=s, bpar=bpar, C=C: e.matmul(
                        pb[bpar][:, s * 128:(s + 1) * 128],
                        castbuf[par][:, s * C + c * 128: s * C + (c + 1) * 128],
                        identb[:], start=True, stop=True)
                if s == nstrips - 1:
                    t_tr = ph.op("tensor", fn, tr)
                else:
                    ph.op("tensor", fn)
            ph.wait("vector", tr, t_tr)
            ev_end = ph.op("vector", lambda e, c=c, bpar=bpar, q=q, par=par: e.tensor_copy(
                stage[par][:, c * q:(c + 1) * q], pb[bpar][:, 0:q]), ev)
            bank_guard[bpar] = ev_end
        castbuf_guard["small" if is_small else par] = (tr, tr.v)
        ph.wait("scalar", ev, ev_end)
        stv = stage[par][:, 0:CB * q].rearrange("p (c q) -> p c q", c=CB)
        t_st = ph.dma("scalar", dst_store_out, stv, stg[par])
        stage_guard[par] = t_st
        gi += 1

    for (src, dst) in transpose_jobs:
        R, C = src.shape
        CB = C // 128
        RT = R // 128
        s0 = 0
        while s0 < RT:
            ns = min(8, RT - s0, 8192 // C)
            src_grp = src[s0 * 128:(s0 + ns) * 128, :].rearrange("(s p) c -> p s c", p=128)
            dst_out = dst.rearrange("(c p) r -> p c r", p=128)[:, :, s0 * 128: s0 * 128 + ns * 128]
            do_group(src_grp, ns, CB, dst_out)
            s0 += ns

    for (src, dst) in small_jobs:
        r, C = src.shape
        CB = C // 128
        dst_out = dst.rearrange("(c p) r -> p c r", p=128)
        do_group(src, 1, CB, dst_out, is_small=True, rsmall=r)

    for (src, dst) in cast_jobs:
        R, C = src.shape
        RT = R // 128
        s0 = 0
        while s0 < RT:
            ns = min(8 * 1024 // C, RT - s0)
            par = gi % 2
            wait2("gpsimd", castbuf_guard.get(par))
            view = castbuf[par][:, 0:ns * C].rearrange("p (s c) -> p s c", s=ns)
            src_grp = src[s0 * 128:(s0 + ns) * 128, :].rearrange("(s p) c -> p s c", p=128)
            t_ld = ph.dma("gpsimd", view, src_grp, ld[par])
            ph.wait("sync", ld[par], t_ld)
            dst_out = dst[s0 * 128:(s0 + ns) * 128, :].rearrange("(s p) c -> p s c", p=128)
            t_st = ph.dma("sync", dst_out, view, stc[par])
            castbuf_guard[par] = (stc[par], t_st)
            gi += 1
            s0 += ns

    for s_ in (stg[0], stg[1], stc[0], stc[1]):
        ph.wait("sync", s_, s_.v)
        ph.wait("scalar", s_, s_.v)


PREP_OUTS = [
    ("wq0b", [D, D]), ("wk0b", [D, D]), ("wv0b", [D, D]),
    ("gwT", [D, E0]),
    ("w1T0", [E0, D, I0]), ("w3T0", [E0, D, I0]), ("w2T0", [E0, I0, D]),
    ("ws1T", [D, I0]), ("ws3T", [D, I0]), ("ws2T", [I0, D]),
    ("q1T", [D, D]), ("k1T", [D, D]), ("v1T", [D, D]),
    ("r1T", [D, E1]),
    ("w1T1", [E1, D, H1]), ("w2T1", [E1, H1, D]),
]

PREP_INS = [
    ("L0_attn_weight_q", [D, D]), ("L0_attn_weight_k", [D, D]), ("L0_attn_weight_v", [D, D]),
    ("L0_ffn_gate_w", [E0, D]),
    ("L0_ffn_w1", [E0, I0, D]), ("L0_ffn_w3", [E0, I0, D]), ("L0_ffn_w2", [E0, D, I0]),
    ("L0_ffn_ws1", [I0, D]), ("L0_ffn_ws3", [I0, D]), ("L0_ffn_ws2", [D, I0]),
    ("L1_attn_Wq", [D, D]), ("L1_attn_Wk", [D, D]), ("L1_attn_Wv", [D, D]),
    ("L1_ffn_router_w", [E1, D]),
    ("L1_ffn_w1", [E1, H1, D]), ("L1_ffn_w2", [E1, D, H1]),
]


def prep_job_lists(src, dst):
    cj = [(src["L0_attn_weight_q"], dst["wq0b"]),
          (src["L0_attn_weight_k"], dst["wk0b"]),
          (src["L0_attn_weight_v"], dst["wv0b"])]
    tj0 = []
    for e in range(E0):
        tj0.append((src["L0_ffn_w1"][e], dst["w1T0"][e]))
        tj0.append((src["L0_ffn_w3"][e], dst["w3T0"][e]))
        tj0.append((src["L0_ffn_w2"][e], dst["w2T0"][e]))
    tj0 += [(src["L0_ffn_ws1"], dst["ws1T"]), (src["L0_ffn_ws3"], dst["ws3T"]),
            (src["L0_ffn_ws2"], dst["ws2T"])]
    tj1 = [(src["L1_attn_Wq"], dst["q1T"]), (src["L1_attn_Wk"], dst["k1T"]),
           (src["L1_attn_Wv"], dst["v1T"])]
    for e in range(E1):
        tj1.append((src["L1_ffn_w1"][e], dst["w1T1"][e]))
        tj1.append((src["L1_ffn_w2"][e], dst["w2T1"][e]))
    sj = [(src["L0_ffn_gate_w"], dst["gwT"]), (src["L1_ffn_router_w"], dst["r1T"])]
    return cj, tj0, tj1, sj


def prep_jobs(ph, src, dst):
    cj, tj0, tj1, sj = prep_job_lists(src, dst)
    emit_prep(ph, tj0 + tj1, cj, sj)


def build_prep_program():
    nc = bass.Bass("TRN2", target_bir_lowering=False, debug=False)
    src = {n: nc.dram_tensor(n, s, F32, kind="ExternalInput").ap() for n, s in PREP_INS}
    dst = {n: nc.dram_tensor(n, s, BF16, kind="ExternalOutput").ap() for n, s in PREP_OUTS}
    es = ExitStack()
    with es:
        ph = Phase(nc, es, "prep_")
        prep_jobs(ph, src, dst)
        ph.emit()
    return nc



def emit_layer_a1(ph, x_in, wq, wk, wv, xnT_out, qT_out, kT_sb, v_sb, biases=None,
                  bg_dmas=None):
    nc = ph.nc
    G = T // 512
    xbuf = [ph.sb(f"a1x{i}", [128, D], F32) for i in range(2)]
    xnln = [ph.sb(f"a1xn{i}", [128, D], BF16) for i in range(4)]
    xnT_g = ph.sb("a1xnTg", [128, DC * 512], BF16)
    qT_g = ph.sb("a1qTg", [128, DC * 512], BF16)
    wbuf = [ph.sb(f"a1w{i}", [128, DC * D], BF16) for i in range(2)]
    identb = ph.sb("a1ident", [128, 128], BF16)
    onesf = ph.sb("a1ones", [128, 128], F32)
    r1 = ph.sb("a1r1", [128, 1], F32)
    r2 = ph.sb("a1r2", [128, 1], F32)
    mcol = ph.sb("a1m", [128, 1], F32)
    vcol = ph.sb("a1v", [128, 1], F32)
    m2col = ph.sb("a1m2", [128, 1], F32)
    rstd = ph.sb("a1rstd", [128, 1], F32)
    has_bias = biases is not None
    if has_bias:
        bq_sb = ph.sb("a1bq", [128, DC], F32)
        bk_sb = ph.sb("a1bk", [128, DC], F32)
        bv_sb = ph.sb("a1bvc", [128, DC], F32)
        bvb = ph.sb("a1bvb", [128, D], BF16)
    ps_xT = ph.ps("a1psxT", [128, D], F32)
    ps_sq = ph.ps("a1pssq", [128, D], F32)
    ps_mm = [ph.ps(f"a1psmm{i}", [128, 512], F32) for i in range(4)]

    ldx = [ph.sem("ldx0"), ph.sem("ldx1")]
    ldb = ph.sem("ldb")
    wl = [ph.sem("wl0"), ph.sem("wl1")]
    soxn = ph.sem("soxn")
    soq = ph.sem("soq")
    sp = ph.sem("sp")
    sv = ph.esem("vector")
    sa = ph.esem("scalar")
    sg = ph.esem("gpsimd")

    t_on = ph.op("gpsimd", lambda e: e.memset(onesf[:], 1.0), sg)
    ph.wait("gpsimd", sg, t_on)
    t_id = ph.op("gpsimd", lambda e: e.affine_select(
        identb[:], onesf[:], [[-1, 128]], ALU.is_equal, 0.0, channel_multiplier=1), sg)
    ph.wait("tensor", sg, t_id)
    if has_bias:
        bq, bk, bv = biases
        for c in range(DC):
            ph.dma("sync", bq_sb[:, c:c + 1], bq[c * 128:(c + 1) * 128].unsqueeze(1), ldb)
            ph.dma("sync", bk_sb[:, c:c + 1], bk[c * 128:(c + 1) * 128].unsqueeze(1), ldb)
            ph.dma("sync", bv_sb[:, c:c + 1], bv[c * 128:(c + 1) * 128].unsqueeze(1), ldb)
        bvrow = ph.sb("a1bvrow", [1, D], F32)
        ph.dma("sync", bvrow[:], bv.unsqueeze(0), ldb)
        ph.wait("tensor", ldb, ldb.v)
        ph.wait("scalar", ldb, ldb.v)
        for n in range(2):
            t_bb = ph.op("tensor", lambda e, n=n: e.matmul(
                ps_sq[:, n * 512:(n + 1) * 512], onesf[0:1, :], bvrow[:, n * 512:(n + 1) * 512],
                start=True, stop=True), sp)
        ph.wait("vector", sp, t_bb)
        t_bv = ph.op("vector", lambda e: e.tensor_copy(bvb[:], ps_sq[:]), sv)
        ph.wait("scalar", sv, t_bv)

    wbuf_guard = {}
    psum_guard = {}
    W_APS = [wq, wk, wv]

    def load_w(gi_w, par):
        src = W_APS[gi_w % 3]
        g = wbuf_guard.get(par)
        if g is not None:
            ph.wait("gpsimd", sp, g)
        return ph.dma("gpsimd", wbuf[par][:].rearrange("p (c d) -> p c d", c=DC),
                      src.rearrange("(c p) d -> p c d", p=128), wl[par])

    wq_t = load_w(0, 0)
    wk_t = load_w(1, 1)
    wloads = [wq_t, wk_t]
    wi = 2
    bgsem = ph.sem("bg")
    bg_dmas = list(bg_dmas or [])
    bg_i = 0

    def pump_bg(n):
        nonlocal bg_i
        for _ in range(n):
            if bg_i < len(bg_dmas):
                dst_, src_ = bg_dmas[bg_i]
                ph.dma("gpsimd", dst_, src_, bgsem)
                bg_i += 1

    t_xn_done = {}
    qkv_done_prev = None
    st_xn_prev = None
    st_q_prev = None
    xbuf_guard = {}
    xnln_guard = {}

    for g in range(G):
        t_xns = {}
        for tau in range(4):
            i = g * 4 + tau
            par = i % 2
            if xbuf_guard.get(par) is not None:
                ph.wait("sync", sv, xbuf_guard[par])
            t_x = ph.dma("sync", xbuf[par][:], x_in[i * 128:(i + 1) * 128, :], ldx[par])
            ph.wait("vector", ldx[par], t_x)
            t_r1 = ph.op("vector", lambda e, par=par: e.reduce_sum(r1[:], xbuf[par][:], axis=AX.X), sv)
            ph.wait("scalar", ldx[par], t_x)
            ph.wait("scalar", sv, t_r1)
            t_sq = ph.op("scalar", lambda e, par=par: e.activation(
                ps_sq[:], xbuf[par][:], AF.Square, accum_out=r2[:]), sa)
            ph.wait("vector", sa, t_sq)
            ph.op("vector", lambda e: e.tensor_scalar(mcol[:], r1[:], 1.0 / D, None, ALU.mult))
            ph.op("vector", lambda e: e.tensor_scalar(vcol[:], r2[:], 1.0 / D, float(LN_EPS),
                                                      ALU.mult, ALU.add))
            ph.op("vector", lambda e: e.tensor_tensor(m2col[:], mcol[:], mcol[:], ALU.mult))
            t_var = ph.op("vector", lambda e: e.tensor_tensor(vcol[:], vcol[:], m2col[:], ALU.subtract), sv)
            ph.wait("scalar", sv, t_var)
            ph.op("scalar", lambda e: e.activation(vcol[:], vcol[:], AF.Ln))
            t_rstd = ph.op("scalar", lambda e: e.activation(rstd[:], vcol[:], AF.Exp, scale=-0.5), sa)
            ph.wait("vector", sa, t_rstd)
            if xnln_guard.get(tau) is not None:
                ph.wait("vector", sp, xnln_guard[tau])
            t_xn = ph.op("vector", lambda e, par=par, tau=tau: e.tensor_scalar(
                xnln[tau][:], xbuf[par][:], mcol[:], rstd[:], ALU.subtract, ALU.mult), sv)
            t_xns[tau] = t_xn
            xbuf_guard[par] = t_xn
        for tau in range(4):
            i = g * 4 + tau
            ph.wait("tensor", sv, t_xns[tau])
            if tau == 0 and qkv_done_prev is not None:
                ph.wait("vector", sp, qkv_done_prev)
                ph.wait("vector", soxn, st_xn_prev)
            for c in range(DC):
                fn = lambda e, c=c, tau=tau: e.matmul(
                    ps_xT[:, c * 128:(c + 1) * 128], xnln[tau][:, c * 128:(c + 1) * 128],
                    identb[:], start=True, stop=True)
                if c == DC - 1:
                    t_tr = ph.op("tensor", fn, sp)
                else:
                    ph.op("tensor", fn)
            xnln_guard[tau] = t_tr
            ph.wait("vector", sp, t_tr)
            t_ev = ph.op("vector", lambda e, tau=tau: e.tensor_copy(
                xnT_g[:].rearrange("p (c t) -> p c t", c=DC)[:, :, tau * 128:(tau + 1) * 128],
                ps_xT[:].rearrange("p (c t) -> p c t", c=DC)), sv)
            t_xn_done[i] = t_ev
            ph.wait("tensor", sv, t_ev)

        ph.wait("sync", sv, t_xn_done[g * 4 + 3])
        t_st_xn = ph.dma("sync", xnT_out.rearrange("(c p) t -> p c t", p=128)[:, :, g * 512:(g + 1) * 512],
                         xnT_g[:].rearrange("p (c t) -> p c t", c=DC), soxn)
        st_xn_prev = t_st_xn

        grp_ready = t_xn_done[g * 4 + 3]
        ph.wait("tensor", sv, grp_ready)
        for proj in range(3):
            wpar = (g * 3 + proj) % 2
            ph.wait("tensor", wl[wpar], wloads[g * 3 + proj])
            if proj < 2:
                for m in range(DC):
                    pidx = m % 4
                    pm = ps_mm[pidx]
                    if psum_guard.get(pidx) is not None:
                        gs, gt = psum_guard[pidx]
                        ph.wait("tensor", gs, gt)
                    for c in range(DC):
                        fn = lambda e, m=m, c=c, pm=pm, wpar=wpar: e.matmul(
                            pm[:], wbuf[wpar][:, c * D + m * 128: c * D + (m + 1) * 128],
                            xnT_g[:, c * 512:(c + 1) * 512],
                            start=(c == 0), stop=(c == DC - 1))
                        if c == DC - 1:
                            t_mm = ph.op("tensor", fn, sp)
                        else:
                            ph.op("tensor", fn)
                    ph.wait("scalar", sp, t_mm)
                    if proj == 0:
                        if st_q_prev is not None and m == 0:
                            ph.wait("scalar", soq, st_q_prev)
                        dst = qT_g[:, m * 512:(m + 1) * 512]
                        bias_ap = bq_sb[:, m:m + 1] if has_bias else None
                    else:
                        dst = kT_sb[:, m * T + g * 512: m * T + (g + 1) * 512]
                        bias_ap = bk_sb[:, m:m + 1] if has_bias else None
                    if bias_ap is not None:
                        t_ev2 = ph.op("scalar", lambda e, dst=dst, pm=pm, b=bias_ap: e.activation(
                            dst, pm[:], AF.Identity, bias=b), sa)
                    else:
                        t_ev2 = ph.op("scalar", lambda e, dst=dst, pm=pm: e.activation(
                            dst, pm[:], AF.Copy), sa)
                    psum_guard[pidx] = (sa, t_ev2)
            else:
                for tau in range(4):
                    i = g * 4 + tau
                    for n in range(2):
                        pidx = 2 * (tau % 2) + n
                        pm = ps_mm[pidx]
                        if psum_guard.get(pidx) is not None:
                            gs, gt = psum_guard[pidx]
                            ph.wait("tensor", gs, gt)
                        for c in range(DC):
                            fn = lambda e, tau=tau, n=n, c=c, pm=pm, wpar=wpar: e.matmul(
                                pm[:], xnT_g[:, c * 512 + tau * 128: c * 512 + (tau + 1) * 128],
                                wbuf[wpar][:, c * D + n * 512: c * D + (n + 1) * 512],
                                start=(c == 0), stop=(c == DC - 1))
                            if c == DC - 1:
                                t_mm = ph.op("tensor", fn, sp)
                            else:
                                ph.op("tensor", fn)
                        dst = v_sb[:, i * D + n * 512: i * D + (n + 1) * 512]
                        if has_bias:
                            ph.wait("vector", sp, t_mm)
                            t_ev3 = ph.op("vector", lambda e, dst=dst, pm=pm, n=n: e.tensor_tensor(
                                dst, pm[:], bvb[:, n * 512:(n + 1) * 512], ALU.add), sv)
                            psum_guard[pidx] = (sv, t_ev3)
                        else:
                            ph.wait("scalar", sp, t_mm)
                            t_ev2 = ph.op("scalar", lambda e, dst=dst, pm=pm: e.activation(
                                dst, pm[:], AF.Copy), sa)
                            psum_guard[pidx] = (sa, t_ev2)
            wbuf_guard[wpar] = sp.v
            if wi < 3 * G:
                wloads.append(load_w(wi, wi % 2))
                wi += 1
        qkv_done_prev = sp.v
        pump_bg((len(bg_dmas) + G - 1) // G if bg_dmas else 0)
        last_q_evac = sa.v
        ph.wait("sync", sa, last_q_evac)
        st_q_prev = ph.dma("sync", qT_out.rearrange("(c p) t -> p c t", p=128)[:, :, g * 512:(g + 1) * 512],
                           qT_g[:].rearrange("p (c t) -> p c t", c=DC), soq)

    pump_bg(len(bg_dmas))
    ph.wait("gpsimd", bgsem, bgsem.v)
    ph.wait("sync", soxn, soxn.v)
    ph.wait("sync", soq, soq.v)
    ph.wait("sync", sa, sa.v)
    ph.wait("sync", sv, sv.v)



def emit_layer_a2(ph, mode, x_in, qT_in, kT_sb, v_sb, s_out, bg_dmas=None):
    nc = ph.nc
    retnet = (mode == "retnet")
    qtile = [ph.sb(f"a2q{i}", [128, DC * 128], BF16) for i in range(2)]
    xbuf = [ph.sb(f"a2x{i}", [128, D], F32) for i in range(2)]
    sbuf = [ph.sb(f"a2s{i}", [128, D], F32) for i in range(2)]
    probsT = [ph.sb(f"a2pT{i}", [128, 512], BF16) for i in range(2)]
    identb = ph.sb("a2ident", [128, 128], BF16)
    onesf = ph.sb("a2ones", [128, 128], F32)
    if retnet:
        probs = [ph.sb(f"a2probs{i}", [128, WIN * 128], BF16) for i in range(2)]
        Mdec = ph.sb("a2M", [128, (WIN + 1) * 128], F32)
        if32 = ph.sb("a2iota", [128, 128], F32)
        mtmp = ph.sb("a2mtmp", [128, 128], F32)
    else:
        probs = [ph.sb(f"a2probs{i}", [128, T], BF16) for i in range(2)]
        scores = [ph.sb(f"a2scores{i}", [128, T], BF16) for i in range(2)]
        tri = ph.sb("a2tri", [128, 128], BF16)
        rmax = [ph.sb(f"a2rmax{k}", [128, 1], F32) for k in range(2)]
        gmax = [ph.sb(f"a2gmax{k}", [128, 1], F32) for k in range(2)]
        nm32 = [ph.sb(f"a2nm32{k}", [128, 1], F32) for k in range(2)]
        gsum = [ph.sb(f"a2gsum{k}", [128, 12], F32) for k in range(2)]
        rsum = [ph.sb(f"a2rsum{k}", [128, 1], F32) for k in range(2)]
        rcol = [ph.sb(f"a2rcol{k}", [128, 1], F32) for k in range(2)]
    ps_sc = [ph.ps(f"a2pssc{i}", [128, 512], F32) for i in range(2)]
    ps_pt = [ph.ps(f"a2pspt{i}", [128, 512], F32) for i in range(2)]
    ps_out = [ph.ps(f"a2psout{i}", [128, D], F32) for i in range(2)]

    ldq = [ph.sem("ldq0"), ph.sem("ldq1")]
    ldxx = [ph.sem("ldxx0"), ph.sem("ldxx1")]
    sos = [ph.sem("sos0"), ph.sem("sos1")]
    sp = ph.sem("sp")
    sv = ph.esem("vector")
    sa = ph.esem("scalar")
    sg = ph.esem("gpsimd")

    t_on = ph.op("gpsimd", lambda e: e.memset(onesf[:], 1.0), sg)
    ph.wait("gpsimd", sg, t_on)
    t_id = ph.op("gpsimd", lambda e: e.affine_select(
        identb[:], onesf[:], [[-1, 128]], ALU.is_equal, 0.0, channel_multiplier=1), sg)
    ph.wait("tensor", sg, t_id)
    lgam = float(np.log(GAMMA))
    if retnet:
        t_io = ph.op("gpsimd", lambda e: e.iota(
            if32[:], [[-1, 128]], channel_multiplier=1,
            allow_small_or_imprecise_dtypes=True), sg)
        ph.wait("scalar", sg, t_io)
        t_m0 = ph.op("scalar", lambda e: e.activation(mtmp[:], if32[:], AF.Exp, scale=lgam), sa)
        ph.wait("vector", sa, t_m0)
        for d in range(1, WIN):
            gd = float(GAMMA ** (128 * d))
            t_md = ph.op("vector", lambda e, d=d, gd=gd: e.tensor_scalar(
                Mdec[:, d * 128:(d + 1) * 128], mtmp[:], gd, None, ALU.mult), sv)
        ph.wait("gpsimd", sa, t_m0)
        t_msk = ph.op("gpsimd", lambda e: e.affine_select(
            Mdec[:, 0:128], mtmp[:], [[-1, 128]], ALU.is_ge, 0.0,
            channel_multiplier=1), sg)
        ph.wait("vector", sg, t_msk)
    else:
        t_tri = ph.op("gpsimd", lambda e: e.affine_select(
            tri[:], onesf[:], [[-1, 128]], ALU.is_ge, 0.0, channel_multiplier=1), sg)
        ph.wait("vector", sg, t_tri)
        ph.wait("scalar", sg, t_tri)

    psc_guard = {}
    ppt_guard = {}
    probs_guard = {}
    scores_guard = {}
    sbuf_guard = {}
    psout_guard = {}
    last_dg = {}
    state = {}


    def part1(i):
        par = i % 2
        if sbuf_guard.get(("q", par)) is not None:
            gs, gt = sbuf_guard[("q", par)]
            ph.wait("sync", gs, gt)
        t_q = ph.dma("sync", qtile[par][:],
                     qT_in.rearrange("(c p) t -> p c t", p=128)[:, :, i * 128:(i + 1) * 128], ldq[par])
        t_x = ph.dma("sync", xbuf[par][:], x_in[i * 128:(i + 1) * 128, :], ldxx[par])
        ph.wait("tensor", ldq[par], t_q)

        j0 = max(0, i - (WIN - 1)) if retnet else 0
        nk = (i + 1 - j0) * 128
        k0 = j0 * 128
        nspan = (nk + 511) // 512
        st = dict(t_x=t_x, j0=j0, nk=nk)

        pbuf = probs[par]
        if retnet and probs_guard.get(par) is not None:
            ph.wait("vector", sp, probs_guard[par])
        if not retnet:
            sbuf_sc = scores[par]
            if scores_guard.get(par) is not None:
                ph.wait("vector", sa, scores_guard[par])
        for s0 in range(nspan):
            ks = k0 + s0 * 512
            w = min(512, k0 + nk - ks)
            pidx = s0 % 2
            pm = ps_sc[pidx]
            if psc_guard.get(pidx) is not None:
                gs, gt = psc_guard[pidx]
                ph.wait("tensor", gs, gt)
            for c in range(DC):
                fn = lambda e, c=c, pm=pm, par=par, ks=ks, w=w: e.matmul(
                    pm[:, 0:w], qtile[par][:, c * 128:(c + 1) * 128],
                    kT_sb[:, c * T + ks: c * T + ks + w],
                    start=(c == 0), stop=(c == DC - 1))
                if c == DC - 1:
                    t_mm = ph.op("tensor", fn, sp)
                else:
                    ph.op("tensor", fn)
            ph.wait("vector", sp, t_mm)
            if retnet:
                nb = w // 128
                jst = ks // 128
                d0 = i - jst
                t_e = ph.op("vector", lambda e, nb=nb, jst=jst, j0=j0, d0=d0, pidx=pidx, pbuf=pbuf: e.tensor_tensor(
                    pbuf[:, (jst - j0) * 128:(jst - j0 + nb) * 128].rearrange("p (b q) -> p b q", b=nb),
                    ps_sc[pidx][:, 0:nb * 128].rearrange("p (b q) -> p b q", b=nb),
                    bass.AP(Mdec, d0 * 128, [[(WIN + 1) * 128, 128], [-128, nb], [1, 128]]),
                    ALU.mult), sv)
            else:
                ph.wait("scalar", sp, t_mm)
                t_e = ph.op("scalar", lambda e, ks=ks, w=w, pidx=pidx, sbuf_sc=sbuf_sc: e.activation(
                    sbuf_sc[:, ks:ks + w], ps_sc[pidx][:, 0:w], AF.Copy), sa)
            psc_guard[pidx] = (sv, t_e) if retnet else (sa, t_e)
        if retnet:
            st["t_probs_done"] = sv.v
        else:
            ph.wait("vector", sa, t_e)
            ph.op("vector", lambda e, par=par, nk=nk, sbuf_sc=sbuf_sc: e.reduce_max(
                rmax[par][:], sbuf_sc[:, 0:nk], axis=AX.X))
            if last_dg.get(par) is not None:
                ph.wait("vector", sa, last_dg[par])
            t_nm = ph.op("vector", lambda e, par=par: e.tensor_scalar(
                nm32[par][:], rmax[par][:], -1.0 / 32.0, None, ALU.mult), sv)
            ph.wait("scalar", sv, t_nm)
            if probs_guard.get(par) is not None:
                ph.wait("scalar", sp, probs_guard[par])
            ns = 0
            for s0 in range(nspan):
                ks = s0 * 512
                w = min(512, nk - ks)
                wf = w if ks + w <= i * 128 else max(0, i * 128 - ks)
                if wf > 0:
                    ph.op("scalar", lambda e, ks=ks, wf=wf, ns=ns, pbuf=pbuf, sbuf_sc=sbuf_sc, par=par: e.activation(
                        pbuf[:, ks:ks + wf], sbuf_sc[:, ks:ks + wf], AF.Exp,
                        scale=1.0 / 32.0, bias=nm32[par][:], accum_out=gsum[par][:, ns:ns + 1]))
                    ns += 1
            t_dg = ph.op("scalar", lambda e, i=i, pbuf=pbuf, sbuf_sc=sbuf_sc, par=par: e.activation(
                pbuf[:, i * 128:(i + 1) * 128], sbuf_sc[:, i * 128:(i + 1) * 128], AF.Exp,
                scale=1.0 / 32.0, bias=nm32[par][:]), sa)
            scores_guard[par] = t_dg
            last_dg[par] = t_dg
            ph.wait("vector", sa, t_dg)
            t_dm = ph.op("vector", lambda e, i=i, pbuf=pbuf: e.tensor_tensor(
                pbuf[:, i * 128:(i + 1) * 128], pbuf[:, i * 128:(i + 1) * 128],
                tri[:], ALU.mult), sv)
            ph.op("vector", lambda e, i=i, ns=ns, pbuf=pbuf, par=par: e.reduce_sum(
                gsum[par][:, ns:ns + 1], pbuf[:, i * 128:(i + 1) * 128], axis=AX.X))
            ph.op("vector", lambda e, ns=ns, par=par: e.reduce_sum(
                rsum[par][:], gsum[par][:, 0:ns + 1], axis=AX.X))
            t_rc = ph.op("vector", lambda e, par=par: e.reciprocal(rcol[par][:], rsum[par][:]), sv)
            st["t_dm"] = t_dm
            st["t_dg"] = t_dg
            st["t_rc"] = t_rc
        state[i] = st

    def part2(i):
        par = i % 2
        st = state.pop(i)
        j0, nk, t_x = st["j0"], st["nk"], st["t_x"]
        pbuf = probs[par]
        pso = ps_out[par]
        if psout_guard.get(par) is not None:
            gs, gt = psout_guard[par]
            ph.wait("tensor", gs, gt)
        first = True
        nblk = nk // 128
        tgt_out = None
        for b0 in range(0, nblk, 4):
            bn = min(4, nblk - b0)
            tpidx = (b0 // 4) % 2
            if ppt_guard.get(("ps", tpidx)) is not None:
                ph.wait("tensor", sv, ppt_guard[("ps", tpidx)])
            if b0 == 0:
                if retnet:
                    ph.wait("tensor", sv, st["t_probs_done"])
                else:
                    ph.wait("tensor", sv, st["t_dm"])
                    ph.wait("tensor", sa, st["t_dg"])
            for b in range(bn):
                fn = lambda e, b0=b0, b=b, tpidx=tpidx, pbuf=pbuf: e.matmul(
                    ps_pt[tpidx][:, b * 128:(b + 1) * 128],
                    pbuf[:, (b0 + b) * 128:(b0 + b + 1) * 128], identb[:],
                    start=True, stop=True)
                if b == bn - 1:
                    t_tp = ph.op("tensor", fn, sp)
                else:
                    ph.op("tensor", fn)
            if ppt_guard.get(("sb", tpidx)) is not None:
                ph.wait("vector", sp, ppt_guard[("sb", tpidx)])
            ph.wait("vector", sp, t_tp)
            t_pev = ph.op("vector", lambda e, tpidx=tpidx, bn=bn: e.tensor_copy(
                probsT[tpidx][:, 0:bn * 128], ps_pt[tpidx][:, 0:bn * 128]), sv)
            ppt_guard[("ps", tpidx)] = t_pev
            ph.wait("tensor", sv, t_pev)
            for b in range(bn):
                j = j0 + b0 + b
                final = (b0 + b == nblk - 1)
                for n in range(2):
                    last = (b == bn - 1 and n == 1)
                    fn = lambda e, b=b, j=j, n=n, tpidx=tpidx, first=first, final=final, pso=pso: e.matmul(
                        pso[:, n * 512:(n + 1) * 512],
                        probsT[tpidx][:, b * 128:(b + 1) * 128],
                        v_sb[:, j * D + n * 512: j * D + (n + 1) * 512],
                        start=first, stop=final)
                    if last:
                        tgt_out = ph.op("tensor", fn, sp)
                    else:
                        ph.op("tensor", fn)
                first = False
            ppt_guard[("sb", tpidx)] = tgt_out
        probs_guard[par] = tgt_out

        ph.wait("vector", sp, tgt_out)
        ph.wait("vector", ldxx[par], t_x)
        if sbuf_guard.get(("s", par)) is not None:
            gs, gt = sbuf_guard[("s", par)]
            ph.wait("vector", gs, gt)
        if retnet:
            fn = lambda e, par=par, pso=pso: e.tensor_tensor(
                sbuf[par][:], pso[:], xbuf[par][:], ALU.add)
        else:
            fn = lambda e, par=par, pso=pso: e.scalar_tensor_tensor(
                sbuf[par][:], pso[:], rcol[par][:], xbuf[par][:], ALU.mult, ALU.add)
        t_sev = ph.op("vector", fn, sv)
        psout_guard[par] = (sv, t_sev)
        sbuf_guard[("q", par)] = (sv, t_sev)
        ph.wait("sync", sv, t_sev)
        t_st = ph.dma("sync", s_out[i * 128:(i + 1) * 128, :], sbuf[par][:], sos[par])
        sbuf_guard[("s", par)] = (sos[par], t_st)

    bgsem = ph.sem("bg")
    bg_dmas = list(bg_dmas or [])
    bg_i = 0

    def pump_bg(n):
        nonlocal bg_i
        for _ in range(n):
            if bg_i < len(bg_dmas):
                dst_, src_ = bg_dmas[bg_i]
                ph.dma("gpsimd", dst_, src_, bgsem)
                bg_i += 1

    part1(0)
    for i in range(NT):
        if i + 1 < NT:
            part1(i + 1)
        part2(i)
        pump_bg((len(bg_dmas) + NT - 1) // NT if bg_dmas else 0)

    pump_bg(len(bg_dmas))
    ph.wait("gpsimd", bgsem, bgsem.v)
    ph.wait("sync", sos[0], sos[0].v)
    ph.wait("sync", sos[1], sos[1].v)


def build_layer_attn_program(layer):
    nc = bass.Bass("TRN2", target_bir_lowering=False, debug=False)
    x_in = nc.dram_tensor("x", [T, D], F32, kind="ExternalInput").ap()
    if layer == 0:
        wq = nc.dram_tensor("wq0b", [D, D], BF16, kind="ExternalInput").ap()
        wk = nc.dram_tensor("wk0b", [D, D], BF16, kind="ExternalInput").ap()
        wv = nc.dram_tensor("wv0b", [D, D], BF16, kind="ExternalInput").ap()
        biases = None
        mode = "retnet"
    else:
        wq = nc.dram_tensor("q1T", [D, D], BF16, kind="ExternalInput").ap()
        wk = nc.dram_tensor("k1T", [D, D], BF16, kind="ExternalInput").ap()
        wv = nc.dram_tensor("v1T", [D, D], BF16, kind="ExternalInput").ap()
        bq = nc.dram_tensor("bq", [D], F32, kind="ExternalInput").ap()
        bk = nc.dram_tensor("bk", [D], F32, kind="ExternalInput").ap()
        bv = nc.dram_tensor("bv", [D], F32, kind="ExternalInput").ap()
        biases = (bq, bk, bv)
        mode = "softmax"
    xnT_out = nc.dram_tensor("xnT", [D, T], BF16, kind="ExternalOutput").ap()
    qT_hbm = nc.dram_tensor("qT", [D, T], BF16, kind="Internal").ap()
    s_out = nc.dram_tensor("s", [T, D], F32, kind="ExternalOutput").ap()

    dbg = getattr(build_layer_attn_program, "debug", False)
    if dbg:
        dbg_k = nc.dram_tensor("dbg_k", [D, T], BF16, kind="ExternalOutput").ap()
        dbg_v = nc.dram_tensor("dbg_v", [T, D], BF16, kind="ExternalOutput").ap()
        dbg_q = nc.dram_tensor("dbg_q", [D, T], BF16, kind="ExternalOutput").ap()
    with ExitStack() as es:
        kT_sb = es.enter_context(nc.sbuf_tensor("kT", [128, DC * T], BF16))
        v_sb = es.enter_context(nc.sbuf_tensor("v", [128, NT * D], BF16))
        with ExitStack() as es1:
            ph1 = Phase(nc, es1, "a1_", sem_es=es)
            emit_layer_a1(ph1, x_in, wq, wk, wv, xnT_out, qT_hbm, kT_sb, v_sb, biases)
            ph1.emit()
        if dbg:
            with ExitStack() as esd:
                phd = Phase(nc, esd, "dbg_", sem_es=es)
                sod = phd.sem("sod")
                phd.dma("sync", dbg_k.rearrange("(c p) t -> p c t", p=128),
                        kT_sb[:].rearrange("p (c t) -> p c t", c=DC), sod)
                phd.dma("sync", dbg_v.rearrange("(i p) d -> p i d", p=128),
                        v_sb[:].rearrange("p (i d) -> p i d", i=NT), sod)
                phd.dma("sync", dbg_q, qT_hbm, sod)
                phd.wait("sync", sod, sod.v)
                phd.emit()
        with ExitStack() as es2:
            ph2 = Phase(nc, es2, "a2_", sem_es=es)
            emit_layer_a2(ph2, mode, x_in, qT_hbm, kT_sb, v_sb, s_out)
            ph2.emit()
    return nc



def emit_layer_f0(ph, xnT_in, s_in, gwT, gb, w1T, w3T, w2T, ws1T, ws3T, ws2T, x_out):
    nc = ph.nc
    NQ = T // 1024
    IC = I0 // 128
    xnT_q = [ph.sb(f"f0xnT{i}", [128, DC * 1024], BF16) for i in range(2)]
    x_acc = ph.sb("f0xacc", [128, DC * 1024], F32)
    wb1 = [ph.sb(f"f0w1_{i}", [128, DC * I0], BF16) for i in range(2)]
    wb3 = [ph.sb(f"f0w3_{i}", [128, DC * I0], BF16) for i in range(2)]
    wb2 = [ph.sb(f"f0w2_{i}", [128, IC * D], BF16) for i in range(2)]
    gsil = ph.sb("f0gsil", [128, IC * 1024], F32)
    hact = ph.sb("f0hact", [128, IC * 1024], BF16)
    gwT_sb = ph.sb("f0gw", [128, DC * E0], BF16)
    gbrow = ph.sb("f0gbrow", [1, E0], F32)
    onesr = ph.sb("f0ones", [1, 128], F32)
    gbb = ph.sb("f0gbb", [128, E0], F32)
    onescol = ph.sb("f0onec", [128, 1], F32)
    scq = ph.sb("f0scq", [128, 8 * E0], F32)
    comb = ph.sb("f0comb", [128, 8 * E0], F32)
    NTQ = 8
    RW = NTQ * E0
    gbb8 = ph.sb("f0gbb8", [128, RW], F32)
    sbv8 = ph.sb("f0sb8", [128, RW], F32)
    prt8 = ph.sb("f0prt8", [128, 6 * NTQ * 4], F32)
    gs8 = ph.sb("f0gs8", [128, NTQ * 4], F32)
    mn1 = ph.sb("f0mn1", [128, NTQ], F32)
    mn2 = ph.sb("f0mn2", [128, NTQ], F32)
    gm8 = ph.sb("f0gm8", [128, NTQ * 4], F32)
    em8 = ph.sb("f0em8", [128, RW], F32)
    msk8 = ph.sb("f0msk8", [128, RW], F32)
    wm8 = ph.sb("f0wm8", [128, RW], F32)
    sel8 = ph.sb("f0sel8", [128, RW], F32)
    t18 = ph.sb("f0t18", [128, RW], F32)
    mx8 = ph.sb("f0mx8", [128, NTQ], F32)
    rs8 = ph.sb("f0rs8", [128, NTQ], F32)
    rc8 = ph.sb("f0rc8", [128, NTQ], F32)

    ps_r = [ph.ps(f"f0psr{i}", [128, 512], F32) for i in range(2)]
    ps_g = [ph.ps(f"f0psg{i}", [128, 512], F32) for i in range(2)]
    ps_u = [ph.ps(f"f0psu{i}", [128, 512], F32) for i in range(2)]
    ps_y = [ph.ps(f"f0psy{i}", [128, 512], F32) for i in range(2)]

    lda = [ph.sem("lda0"), ph.sem("lda1")]
    ldw = [ph.sem("ldw0"), ph.sem("ldw1")]
    ldc = ph.sem("ldc")
    sox = ph.sem("sox")
    sp = ph.sem("sp")
    sv = ph.esem("vector")
    sa = ph.esem("scalar")
    sg = ph.esem("gpsimd")

    wload_guard = {}
    psum_guard = {}
    ph.dma("sync", gwT_sb[:].rearrange("p (c e) -> p c e", c=DC),
           gwT.rearrange("(c p) e -> p c e", p=128), ldc)
    ph.dma("sync", gbrow[:], gb.unsqueeze(0), ldc)
    t_c = ph.op("gpsimd", lambda e: e.memset(onesr[:], 1.0), sg)
    t_c2 = ph.op("gpsimd", lambda e: e.memset(onescol[:], 1.0), sg)
    ph.wait("tensor", ldc, ldc.v)
    ph.wait("tensor", sg, t_c)
    t_gb = ph.op("tensor", lambda e: e.matmul(ps_r[0][:, 0:E0], onesr[:], gbrow[:],
                                              start=True, stop=True), sp)
    ph.wait("vector", sp, t_gb)
    t_gbb = ph.op("vector", lambda e: e.tensor_copy(gbb[:], ps_r[0][:, 0:E0]), sv)
    for tt in range(8):
        t_gbb = ph.op("vector", lambda e, tt=tt: e.tensor_copy(
            gbb8[:, tt * E0:(tt + 1) * E0], gbb[:]), sv)
    psum_guard_init = (sv, t_gbb)
    ph.wait("vector", sg, t_c2)

    psum_guard[("r", 0)] = psum_guard_init
    W1S = [w1T[e] for e in range(E0)] + [ws1T]
    W3S = [w3T[e] for e in range(E0)] + [ws3T]
    W2S = [w2T[e] for e in range(E0)] + [ws2T]
    NE = E0 + 1

    acc_guard = None
    xnq_guard = {}
    hact_guard = None
    gsil_guard = None
    wloads = {}

    def load_we(ei, par):
        if wload_guard.get(par) is not None:
            ph.wait("gpsimd", sp, wload_guard[par])
        ph.dma("gpsimd", wb1[par][:].rearrange("p (c i) -> p c i", c=DC),
               W1S[ei].rearrange("(c p) i -> p c i", p=128), ldw[par])
        ph.dma("gpsimd", wb3[par][:].rearrange("p (c i) -> p c i", c=DC),
               W3S[ei].rearrange("(c p) i -> p c i", p=128), ldw[par])
        return ph.dma("gpsimd", wb2[par][:].rearrange("p (s d) -> p s d", s=IC),
                      W2S[ei].rearrange("(s p) d -> p s d", p=128), ldw[par])

    wloads[0] = load_we(0, 0)
    wloads[1] = load_we(1, 1)
    nextload = 2

    ldxn = [ph.sem("ldxn0"), ph.sem("ldxn1")]
    xn_loads = {}

    def load_xnq(Q):
        qp = Q % 2
        g = xnq_guard.get(qp)
        if g is not None:
            ph.wait("sync", sp, g)
        xn_loads[Q] = ph.dma("sync", xnT_q[qp][:].rearrange("p (c t) -> p c t", c=DC),
                             xnT_in.rearrange("(c p) t -> p c t", p=128)[:, :, Q * 1024:(Q + 1) * 1024],
                             ldxn[qp])

    load_xnq(0)
    for Q in range(NQ):
        qp = Q % 2
        xq = xnT_q[qp]
        if acc_guard is not None:
            ph.wait("sync", acc_guard[0], acc_guard[1])
        t_s = ph.dma("sync", x_acc[:].rearrange("p (i d) -> p i d", i=DC),
                     s_in[Q * 1024:(Q + 1) * 1024, :].rearrange("(i p) d -> p i d", p=128),
                     lda[1])
        if Q + 1 < NQ:
            load_xnq(Q + 1)
        ph.wait("tensor", ldxn[qp], xn_loads[Q])

        for tau in range(8):
            rp = tau % 2
            if psum_guard.get(("r", rp)) is not None:
                gs_, gt_ = psum_guard[("r", rp)]
                ph.wait("tensor", gs_, gt_)
            for c in range(DC):
                fn = lambda e, c=c, tau=tau, rp=rp, xq=xq: e.matmul(
                    ps_r[rp][:, 0:E0], xq[:, c * 1024 + tau * 128: c * 1024 + (tau + 1) * 128],
                    gwT_sb[:, c * E0:(c + 1) * E0], start=(c == 0), stop=(c == DC - 1))
                if c == DC - 1:
                    t_mm = ph.op("tensor", fn, sp)
                else:
                    ph.op("tensor", fn)
            ph.wait("scalar", sp, t_mm)
            t_sg = ph.op("scalar", lambda e, tau=tau, rp=rp: e.activation(
                scq[:, tau * E0:(tau + 1) * E0], ps_r[rp][:, 0:E0], AF.Sigmoid), sa)
            psum_guard[("r", rp)] = (sa, t_sg)

        ph.wait("vector", sa, t_sg)
        V = lambda fn: ph.op("vector", fn, sv)
        seg = lambda ap, n: ap.rearrange("p (t e) -> p t e", t=n)
        bc8 = lambda ap, n, w: ap.rearrange("p (t u) -> p t u", u=1).broadcast_to([128, n, w])
        V(lambda e: e.tensor_tensor(sbv8[:], scq[:], gbb8[:], ALU.add))
        pairs = [(0, 1), (0, 2), (0, 3), (1, 2), (1, 3), (2, 3)]
        for pi, (a, b) in enumerate(pairs):
            V(lambda e, a=a, b=b, pi=pi: e.tensor_tensor(
                prt8[:, pi * 32:(pi + 1) * 32], sbv8[:, a:RW:4], sbv8[:, b:RW:4], ALU.add))
        V(lambda e: e.tensor_tensor(gs8[:], prt8[:, 0:32], prt8[:, 32:64], ALU.max))
        for pi in range(2, 6):
            V(lambda e, pi=pi: e.tensor_tensor(gs8[:], gs8[:], prt8[:, pi * 32:(pi + 1) * 32], ALU.max))
        for i2, (a, b) in enumerate(pairs):
            dst = mn1 if i2 == 0 else mn2
            V(lambda e, a=a, b=b, dst=dst: e.tensor_tensor(
                dst[:], gs8[:, a:32:4], gs8[:, b:32:4], ALU.min))
            if i2 > 0:
                V(lambda e: e.tensor_tensor(mn1[:], mn1[:], mn2[:], ALU.max))
        V(lambda e: e.tensor_tensor(seg(gm8[:], NTQ), seg(gs8[:], NTQ), bc8(mn1[:], NTQ, 4), ALU.is_ge))
        for c in range(4):
            V(lambda e, c=c: e.tensor_copy(em8[:, c:RW:4], gm8[:]))
        V(lambda e: e.tensor_tensor(t18[:], sbv8[:], em8[:], ALU.mult))
        V(lambda e: e.tensor_scalar(msk8[:], em8[:], 1e30, -1e30, ALU.mult, ALU.add))
        V(lambda e: e.tensor_tensor(msk8[:], msk8[:], t18[:], ALU.add))
        for r in range(4):
            V(lambda e: e.tensor_reduce(mx8[:], seg(msk8[:], NTQ), axis=AX.X, op=ALU.max))
            V(lambda e: e.tensor_tensor(seg(sel8[:], NTQ), seg(msk8[:], NTQ), bc8(mx8[:], NTQ, E0), ALU.is_ge))
            if r == 0:
                V(lambda e: e.tensor_copy(wm8[:], sel8[:]))
            else:
                V(lambda e: e.tensor_tensor(wm8[:], wm8[:], sel8[:], ALU.add))
            if r < 3:
                V(lambda e: e.scalar_tensor_tensor(msk8[:], sel8[:], -2e30, msk8[:], ALU.mult, ALU.add))
        V(lambda e: e.tensor_tensor(t18[:], scq[:], wm8[:], ALU.mult))
        V(lambda e: e.tensor_reduce(rs8[:], seg(t18[:], NTQ), axis=AX.X, op=ALU.add))
        V(lambda e: e.tensor_scalar(rs8[:], rs8[:], 1e-20, None, ALU.add))
        V(lambda e: e.reciprocal(rc8[:], rs8[:]))
        V(lambda e: e.tensor_tensor(seg(comb[:], NTQ), seg(t18[:], NTQ), bc8(rc8[:], NTQ, E0), ALU.mult))
        t_cb = V(lambda e: e.tensor_scalar(comb[:], comb[:], 2.5, None, ALU.mult))

        for ei in range(NE):
            par = (Q * NE + ei) % 2
            ph.wait("tensor", ldw[par], wloads[Q * NE + ei])
            t_hact_last = None
            for s6 in range(IC):
                for tg in range(2):
                    gp = ps_g[(s6 * 2 + tg) % 2]
                    up = ps_u[(s6 * 2 + tg) % 2]
                    key = ("g", (s6 * 2 + tg) % 2)
                    if psum_guard.get(key) is not None:
                        gs_, gt_ = psum_guard[key]
                        ph.wait("tensor", gs_, gt_)
                    for c in range(DC):
                        fn = lambda e, c=c, s6=s6, tg=tg, gp=gp, par=par, xq=xq: e.matmul(
                            gp[:], wb1[par][:, c * I0 + s6 * 128: c * I0 + (s6 + 1) * 128],
                            xq[:, c * 1024 + tg * 512: c * 1024 + (tg + 1) * 512],
                            start=(c == 0), stop=(c == DC - 1))
                        if c == DC - 1:
                            t_gmm = ph.op("tensor", fn, sp)
                        else:
                            ph.op("tensor", fn)
                    key_u = ("u", (s6 * 2 + tg) % 2)
                    if psum_guard.get(key_u) is not None:
                        gs_, gt_ = psum_guard[key_u]
                        ph.wait("tensor", gs_, gt_)
                    for c in range(DC):
                        fn = lambda e, c=c, s6=s6, tg=tg, up=up, par=par, xq=xq: e.matmul(
                            up[:], wb3[par][:, c * I0 + s6 * 128: c * I0 + (s6 + 1) * 128],
                            xq[:, c * 1024 + tg * 512: c * 1024 + (tg + 1) * 512],
                            start=(c == 0), stop=(c == DC - 1))
                        if c == DC - 1:
                            t_umm = ph.op("tensor", fn, sp)
                        else:
                            ph.op("tensor", fn)
                    ph.wait("scalar", sp, t_gmm)
                    if s6 == 0 and tg == 0 and gsil_guard is not None:
                        ph.wait("scalar", sv, gsil_guard)
                    sl = gsil[:, s6 * 1024 + tg * 512: s6 * 1024 + (tg + 1) * 512]
                    hl = hact[:, s6 * 1024 + tg * 512: s6 * 1024 + (tg + 1) * 512]
                    t_sil = ph.op("scalar", lambda e, sl=sl, gp=gp: e.activation(
                        sl, gp[:], AF.Silu), sa)
                    psum_guard[key] = (sa, t_sil)
                    ph.wait("vector", sa, t_sil)
                    ph.wait("vector", sp, t_umm)
                    if s6 == 0 and tg == 0 and hact_guard is not None:
                        ph.wait("vector", sp, hact_guard)
                    t_h = ph.op("vector", lambda e, hl=hl, sl=sl, up=up: e.tensor_tensor(
                        hl, sl, up[:], ALU.mult), sv)
                    psum_guard[key_u] = (sv, t_h)
            gsil_guard = sv.v
            t_hact_done = sv.v
            ph.wait("tensor", sv, t_hact_done)
            ph.wait("vector", lda[1], t_s)
            if Q > 0 and psum_guard.get("acc_ld") is not None:
                pass
            for tau in range(8):
                for n in range(2):
                    yp = ps_y[n]
                    key = ("y", n)
                    if psum_guard.get(key) is not None:
                        gs_, gt_ = psum_guard[key]
                        ph.wait("tensor", gs_, gt_)
                    for s6 in range(IC):
                        fn = lambda e, s6=s6, tau=tau, n=n, yp=yp, par=par: e.matmul(
                            yp[:], hact[:, s6 * 1024 + tau * 128: s6 * 1024 + (tau + 1) * 128],
                            wb2[par][:, s6 * D + n * 512: s6 * D + (n + 1) * 512],
                            start=(s6 == 0), stop=(s6 == IC - 1))
                        if s6 == IC - 1:
                            t_ymm = ph.op("tensor", fn, sp)
                        else:
                            ph.op("tensor", fn)
                    ph.wait("vector", sp, t_ymm)
                    xsl = x_acc[:, tau * 1024 + n * 512: tau * 1024 + (n + 1) * 512]
                    ccol = comb[:, tau * E0 + ei: tau * E0 + ei + 1] if ei < E0 else onescol[:]
                    t_y = ph.op("vector", lambda e, xsl=xsl, yp=yp, ccol=ccol: e.scalar_tensor_tensor(
                        xsl, yp[:], ccol, xsl, ALU.mult, ALU.add), sv)
                    psum_guard[key] = (sv, t_y)
            hact_guard = sp.v
            wload_guard[par] = sp.v
            if nextload < NQ * NE:
                wloads[nextload] = load_we(nextload % NE, nextload % 2)
                nextload += 1
        t_acc_done = sv.v
        ph.wait("sync", sv, t_acc_done)
        t_st = ph.dma("sync", x_out[Q * 1024:(Q + 1) * 1024, :].rearrange("(i p) d -> p i d", p=128),
                      x_acc[:].rearrange("p (i d) -> p i d", i=DC), sox)
        acc_guard = (sox, t_st)
        xnq_guard[qp] = sp.v

    ph.wait("sync", sox, sox.v)



def emit_layer_f1(ph, xnT_in, s_in, r1T, w1T, w2T, x_out):
    nc = ph.nc
    NE8 = T // 512
    HC = H1 // 128
    xnT_e = [ph.sb(f"f1xnT{i}", [128, DC * 512], BF16) for i in range(2)]
    x_acc = ph.sb("f1xacc", [128, 4 * D], F32)
    wb2 = ph.sb("f1w2", [128, HC * D], BF16)
    wb1 = [ph.sb(f"f1w1_{i}", [128, DC * 1024], BF16) for i in range(2)]
    hact = ph.sb("f1hact", [128, HC * 512], BF16)
    r1T_sb = ph.sb("f1r1", [128, DC * E1], BF16)
    RW1 = 4 * E1
    lgt = ph.sb("f1lgt", [128, RW1], F32)
    exp8 = ph.sb("f1exp", [128, RW1], F32)
    pb8 = ph.sb("f1pb", [128, RW1], F32)
    sel8 = ph.sb("f1sel", [128, RW1], F32)
    msk8 = ph.sb("f1msk", [128, RW1], F32)
    wm8 = ph.sb("f1wm", [128, RW1], F32)
    wgt = ph.sb("f1wgt", [128, RW1], F32)
    nm4 = ph.sb("f1nm", [128, 4], F32)
    rs4 = ph.sb("f1rs", [128, 4], F32)
    rc4 = ph.sb("f1rc", [128, 4], F32)
    mx4 = ph.sb("f1mx", [128, 4], F32)

    ps_r = [ph.ps(f"f1psr{i}", [128, 512], F32) for i in range(2)]
    ps_h = [ph.ps(f"f1psh{i}", [128, 512], F32) for i in range(3)]
    ps_y = [ph.ps(f"f1psy{i}", [128, 512], F32) for i in range(2)]

    lda = [ph.sem("lda0"), ph.sem("lda1")]
    ldw1 = [ph.sem("ldw1_0"), ph.sem("ldw1_1")]
    ldw2 = ph.sem("ldw2")
    ldc = ph.sem("ldc")
    sox = ph.sem("sox")
    sp = ph.sem("sp")
    sv = ph.esem("vector")
    sa = ph.esem("scalar")

    ph.dma("sync", r1T_sb[:].rearrange("p (c e) -> p c e", c=DC),
           r1T.rearrange("(c p) e -> p c e", p=128), ldc)

    wload_guard = {}
    w2_guard = None
    psum_guard = {}
    acc_guard = None
    xne_guard = {}
    hact_guard = None
    wloads = {}
    w2loads = {}
    ldxn = [ph.sem("ldxn0"), ph.sem("ldxn1")]
    xn_loads = {}

    def load_w1(k, par):
        ei = (k // 4) % E1
        c = k % 4
        if wload_guard.get(par) is not None:
            ph.wait("gpsimd", sp, wload_guard[par])
        return ph.dma("gpsimd", wb1[par][:].rearrange("p (c2 h) -> p c2 h", c2=DC),
                      w1T[ei][:, c * 1024:(c + 1) * 1024].rearrange("(c2 p) h -> p c2 h", p=128),
                      ldw1[par])

    def load_w2(ke):
        if w2_guard is not None:
            ph.wait("scalar", sp, w2_guard) if False else None
            ph.wait("gpsimd", sp, w2_guard)
        return ph.dma("gpsimd", wb2[:].rearrange("p (s d) -> p s d", s=HC),
                      w2T[ke % E1].rearrange("(s p) d -> p s d", p=128), ldw2)

    wloads[0] = load_w1(0, 0)
    wloads[1] = load_w1(1, 1)
    w2loads[0] = load_w2(0)
    next1 = 2
    next2 = 1

    def load_xne(Q):
        qp = Q % 2
        g = xne_guard.get(qp)
        if g is not None:
            ph.wait("sync", sp, g)
        xn_loads[Q] = ph.dma("sync", xnT_e[qp][:].rearrange("p (c t) -> p c t", c=DC),
                             xnT_in.rearrange("(c p) t -> p c t", p=128)[:, :, Q * 512:(Q + 1) * 512],
                             ldxn[qp])

    load_xne(0)
    for Q in range(NE8):
        qp = Q % 2
        xq = xnT_e[qp]
        if acc_guard is not None:
            ph.wait("sync", acc_guard[0], acc_guard[1])
        t_s = ph.dma("sync", x_acc[:].rearrange("p (i d) -> p i d", i=4),
                     s_in[Q * 512:(Q + 1) * 512, :].rearrange("(i p) d -> p i d", p=128),
                     lda[1])
        if Q + 1 < NE8:
            load_xne(Q + 1)
        ph.wait("tensor", ldxn[qp], xn_loads[Q])
        if Q == 0:
            ph.wait("tensor", ldc, ldc.v)

        for tau in range(4):
            rp = tau % 2
            if psum_guard.get(("r", rp)) is not None:
                gs_, gt_ = psum_guard[("r", rp)]
                ph.wait("tensor", gs_, gt_)
            for c in range(DC):
                fn = lambda e, c=c, tau=tau, rp=rp, xq=xq: e.matmul(
                    ps_r[rp][:, 0:E1], xq[:, c * 512 + tau * 128: c * 512 + (tau + 1) * 128],
                    r1T_sb[:, c * E1:(c + 1) * E1], start=(c == 0), stop=(c == DC - 1))
                if c == DC - 1:
                    t_mm = ph.op("tensor", fn, sp)
                else:
                    ph.op("tensor", fn)
            ph.wait("scalar", sp, t_mm)
            lg_t = lgt[:, tau * E1:(tau + 1) * E1]
            t_lg = ph.op("scalar", lambda e, lg_t=lg_t, rp=rp: e.activation(lg_t, ps_r[rp][:, 0:E1], AF.Copy), sa)
            psum_guard[("r", rp)] = (sa, t_lg)

        ph.wait("vector", sa, t_lg)
        V = lambda fn: ph.op("vector", fn, sv)
        seg = lambda ap: ap.rearrange("p (t e) -> p t e", t=4)
        bc4 = lambda ap, w: ap.rearrange("p (t u) -> p t u", u=1).broadcast_to([128, 4, w])
        t_nm = V(lambda e: e.tensor_reduce(nm4[:], seg(lgt[:]), axis=AX.X, op=ALU.max, negate=True))
        V(lambda e: e.tensor_tensor(seg(exp8[:]), seg(lgt[:]), bc4(nm4[:], E1), ALU.add))
        ph.wait("scalar", sv, sv.v)
        t_ex = ph.op("scalar", lambda e: e.activation(exp8[:], exp8[:], AF.Exp), sa)
        ph.wait("vector", sa, t_ex)
        V(lambda e: e.tensor_reduce(rs4[:], seg(exp8[:]), axis=AX.X, op=ALU.add))
        V(lambda e: e.reciprocal(rc4[:], rs4[:]))
        V(lambda e: e.tensor_tensor(seg(pb8[:]), seg(exp8[:]), bc4(rc4[:], E1), ALU.mult))
        V(lambda e: e.tensor_reduce(mx4[:], seg(pb8[:]), axis=AX.X, op=ALU.max))
        V(lambda e: e.tensor_tensor(seg(sel8[:]), seg(pb8[:]), bc4(mx4[:], E1), ALU.is_ge))
        V(lambda e: e.scalar_tensor_tensor(msk8[:], sel8[:], -2.0, pb8[:], ALU.mult, ALU.add))
        V(lambda e: e.tensor_reduce(mx4[:], seg(msk8[:]), axis=AX.X, op=ALU.max))
        V(lambda e: e.tensor_tensor(seg(msk8[:]), seg(msk8[:]), bc4(mx4[:], E1), ALU.is_ge))
        V(lambda e: e.tensor_tensor(wm8[:], sel8[:], msk8[:], ALU.add))
        V(lambda e: e.tensor_tensor(wgt[:], pb8[:], wm8[:], ALU.mult))

        for ei in range(E1):
            ke = Q * E1 + ei
            for c in range(4):
                kk = ke * 4 + c
                par = kk % 2
                ph.wait("tensor", ldw1[par], wloads[kk])
                for s8 in range(DC):
                    hp = ps_h[(c * DC + s8) % 3]
                    key = ("h", (c * DC + s8) % 3)
                    if psum_guard.get(key) is not None:
                        gs_, gt_ = psum_guard[key]
                        ph.wait("tensor", gs_, gt_)
                    for dc in range(DC):
                        fn = lambda e, dc=dc, s8=s8, hp=hp, par=par, xq=xq: e.matmul(
                            hp[:], wb1[par][:, dc * 1024 + s8 * 128: dc * 1024 + (s8 + 1) * 128],
                            xq[:, dc * 512:(dc + 1) * 512],
                            start=(dc == 0), stop=(dc == DC - 1))
                        if dc == DC - 1:
                            t_hm = ph.op("tensor", fn, sp)
                        else:
                            ph.op("tensor", fn)
                    ph.wait("scalar", sp, t_hm)
                    if c == 0 and s8 == 0 and hact_guard is not None:
                        ph.wait("scalar", sp, hact_guard)
                    hl = hact[:, (c * DC + s8) * 512: (c * DC + s8 + 1) * 512]
                    t_g = ph.op("scalar", lambda e, hl=hl, hp=hp: e.activation(
                        hl, hp[:], AF.Gelu), sa)
                    psum_guard[key] = (sa, t_g)
                wload_guard[par] = sp.v
                if next1 < NE8 * E1 * 4:
                    wloads[next1] = load_w1(next1, next1 % 2)
                    next1 += 1
            t_h_done = sa.v
            ph.wait("tensor", sa, t_h_done)
            ph.wait("tensor", ldw2, w2loads[ke])
            ph.wait("vector", lda[1], t_s)
            for tau in range(4):
                for n in range(2):
                    yp = ps_y[n]
                    key = ("y", n)
                    if psum_guard.get(key) is not None:
                        gs_, gt_ = psum_guard[key]
                        ph.wait("tensor", gs_, gt_)
                    for s in range(HC):
                        fn = lambda e, s=s, tau=tau, n=n, yp=yp: e.matmul(
                            yp[:], hact[:, s * 512 + tau * 128: s * 512 + (tau + 1) * 128],
                            wb2[:, s * D + n * 512: s * D + (n + 1) * 512],
                            start=(s == 0), stop=(s == HC - 1))
                        if s == HC - 1:
                            t_ym = ph.op("tensor", fn, sp)
                        else:
                            ph.op("tensor", fn)
                    ph.wait("vector", sp, t_ym)
                    xsl = x_acc[:, tau * D + n * 512: tau * D + (n + 1) * 512]
                    wcol = wgt[:, tau * E1 + ei: tau * E1 + ei + 1]
                    t_y = ph.op("vector", lambda e, xsl=xsl, yp=yp, wcol=wcol: e.scalar_tensor_tensor(
                        xsl, yp[:], wcol, xsl, ALU.mult, ALU.add), sv)
                    psum_guard[key] = (sv, t_y)
            hact_guard = sp.v
            w2_guard = sp.v
            if next2 < NE8 * E1:
                w2loads[next2] = load_w2(next2)
                next2 += 1
        t_acc_done = sv.v
        ph.wait("sync", sv, t_acc_done)
        t_st = ph.dma("sync", x_out[Q * 512:(Q + 1) * 512, :].rearrange("(i p) d -> p i d", p=128),
                      x_acc[:].rearrange("p (i d) -> p i d", i=4), sox)
        acc_guard = (sox, t_st)
        xne_guard[qp] = sp.v

    ph.wait("sync", sox, sox.v)


def build_layer_f_program(layer):
    nc = bass.Bass("TRN2", target_bir_lowering=False, debug=False)
    xnT = nc.dram_tensor("xnT", [D, T], BF16, kind="ExternalInput").ap()
    s_in = nc.dram_tensor("s", [T, D], F32, kind="ExternalInput").ap()
    x_out = nc.dram_tensor("xout", [T, D], F32, kind="ExternalOutput").ap()
    with ExitStack() as es:
        ph = Phase(nc, es, f"f{layer}_")
        if layer == 0:
            gwT = nc.dram_tensor("gwT", [D, E0], BF16, kind="ExternalInput").ap()
            gb = nc.dram_tensor("gb", [E0], F32, kind="ExternalInput").ap()
            w1T = nc.dram_tensor("w1T0", [E0, D, I0], BF16, kind="ExternalInput").ap()
            w3T = nc.dram_tensor("w3T0", [E0, D, I0], BF16, kind="ExternalInput").ap()
            w2T = nc.dram_tensor("w2T0", [E0, I0, D], BF16, kind="ExternalInput").ap()
            ws1T = nc.dram_tensor("ws1T", [D, I0], BF16, kind="ExternalInput").ap()
            ws3T = nc.dram_tensor("ws3T", [D, I0], BF16, kind="ExternalInput").ap()
            ws2T = nc.dram_tensor("ws2T", [I0, D], BF16, kind="ExternalInput").ap()
            emit_layer_f0(ph, xnT, s_in, gwT, gb, w1T, w3T, w2T, ws1T, ws3T, ws2T, x_out)
        else:
            r1T = nc.dram_tensor("r1T", [D, E1], BF16, kind="ExternalInput").ap()
            w1T = nc.dram_tensor("w1T1", [E1, D, H1], BF16, kind="ExternalInput").ap()
            w2T = nc.dram_tensor("w2T1", [E1, H1, D], BF16, kind="ExternalInput").ap()
            emit_layer_f1(ph, xnT, s_in, r1T, w1T, w2T, x_out)
        ph.emit()
    return nc



_PROG_CACHE = {}


def _prog(key, builder):
    if key not in _PROG_CACHE:
        _PROG_CACHE[key] = builder()
    return _PROG_CACHE[key]


def kernel(**inputs):
    return kernel_fused(**inputs)


def kernel_multilaunch(**inputs):
    x = np.asarray(inputs["L0_attn_x"])
    assert x.shape == (NCORES, T, D)
    shards = [np.ascontiguousarray(x[i]) for i in range(NCORES)]
    cores = list(range(NCORES))

    prep_ins = {n: np.ascontiguousarray(np.asarray(inputs[n])) for n, _ in PREP_INS}
    nc_p = _prog("prep", build_prep_program)
    pw = run_bass_kernel_spmd(nc_p, [prep_ins] * NCORES, core_ids=cores).results

    nc_a0 = _prog("a0", lambda: build_layer_attn_program(0))
    ins = [dict(x=shards[i], wq0b=pw[i]["wq0b"], wk0b=pw[i]["wk0b"], wv0b=pw[i]["wv0b"])
           for i in range(NCORES)]
    ra0 = run_bass_kernel_spmd(nc_a0, ins, core_ids=cores).results

    nc_f0 = _prog("f0", lambda: build_layer_f_program(0))
    gb = np.ascontiguousarray(np.asarray(inputs["L0_ffn_gate_bias"]))
    ins = [dict(xnT=ra0[i]["xnT"], s=ra0[i]["s"], gwT=pw[i]["gwT"], gb=gb,
                w1T0=pw[i]["w1T0"], w3T0=pw[i]["w3T0"], w2T0=pw[i]["w2T0"],
                ws1T=pw[i]["ws1T"], ws3T=pw[i]["ws3T"], ws2T=pw[i]["ws2T"])
           for i in range(NCORES)]
    rf0 = run_bass_kernel_spmd(nc_f0, ins, core_ids=cores).results

    nc_a1 = _prog("a1", lambda: build_layer_attn_program(1))
    bq = np.ascontiguousarray(np.asarray(inputs["L1_attn_bq"]))
    bk = np.ascontiguousarray(np.asarray(inputs["L1_attn_bk"]))
    bv = np.ascontiguousarray(np.asarray(inputs["L1_attn_bv"]))
    ins = [dict(x=rf0[i]["xout"], q1T=pw[i]["q1T"], k1T=pw[i]["k1T"], v1T=pw[i]["v1T"],
                bq=bq, bk=bk, bv=bv) for i in range(NCORES)]
    ra1 = run_bass_kernel_spmd(nc_a1, ins, core_ids=cores).results

    nc_f1 = _prog("f1", lambda: build_layer_f_program(1))
    ins = [dict(xnT=ra1[i]["xnT"], s=ra1[i]["s"], r1T=pw[i]["r1T"],
                w1T1=pw[i]["w1T1"], w2T1=pw[i]["w2T1"]) for i in range(NCORES)]
    rf1 = run_bass_kernel_spmd(nc_f1, ins, core_ids=cores).results

    return np.stack([rf1[i]["xout"] for i in range(NCORES)]).astype(np.float32)



def build_full_program():
    nc = bass.Bass("TRN2", target_bir_lowering=False, debug=False)
    x_in = nc.dram_tensor("x", [T, D], F32, kind="ExternalInput").ap()
    src = {n: nc.dram_tensor(n, s, F32, kind="ExternalInput").ap() for n, s in PREP_INS}
    gb = nc.dram_tensor("L0_ffn_gate_bias", [E0], F32, kind="ExternalInput").ap()
    bq = nc.dram_tensor("L1_attn_bq", [D], F32, kind="ExternalInput").ap()
    bk = nc.dram_tensor("L1_attn_bk", [D], F32, kind="ExternalInput").ap()
    bv = nc.dram_tensor("L1_attn_bv", [D], F32, kind="ExternalInput").ap()
    out = nc.dram_tensor("out", [T, D], F32, kind="ExternalOutput").ap()

    pw = {n: nc.dram_tensor("pw_" + n, s, BF16, kind="Internal").ap() for n, s in PREP_OUTS}
    xnT0 = nc.dram_tensor("xnT0", [D, T], BF16, kind="Internal").ap()
    qT0 = nc.dram_tensor("qT0", [D, T], BF16, kind="Internal").ap()
    s0 = nc.dram_tensor("s0", [T, D], F32, kind="Internal").ap()
    x1 = nc.dram_tensor("x1", [T, D], F32, kind="Internal").ap()
    xnT1 = nc.dram_tensor("xnT1", [D, T], BF16, kind="Internal").ap()
    qT1 = nc.dram_tensor("qT1", [D, T], BF16, kind="Internal").ap()
    s1 = nc.dram_tensor("s1", [T, D], F32, kind="Internal").ap()

    cj, tj0, tj1, sj = prep_job_lists(src, pw)
    stage_of = {}
    bgl = []
    for (s_ap, d_ap) in tj0 + tj1:
        nmb = "st_" + d_ap.tensor.name if hasattr(d_ap, "tensor") else None
        h = nc.dram_tensor(f"bstg{len(bgl)}", list(s_ap.shape), BF16, kind="Internal").ap()
        stage_of[id(s_ap)] = h
        bgl.append((h, s_ap))
    tj0b = [(stage_of[id(s_ap)], d_ap) for (s_ap, d_ap) in tj0]
    tj1b = [(stage_of[id(s_ap)], d_ap) for (s_ap, d_ap) in tj1]
    with ExitStack() as es:
        with ExitStack() as esp:
            php = Phase(nc, esp, "p_", sem_es=es)
            emit_prep(php, [], cj, sj)
            php.emit()
        with ExitStack() as esl:
            kT_sb = esl.enter_context(nc.sbuf_tensor("kT0", [128, DC * T], BF16))
            v_sb = esl.enter_context(nc.sbuf_tensor("v0", [128, NT * D], BF16))
            nh = len(bgl) // 3
            with ExitStack() as e1:
                ph = Phase(nc, e1, "a10_", sem_es=es)
                emit_layer_a1(ph, x_in, pw["wq0b"], pw["wk0b"], pw["wv0b"],
                              xnT0, qT0, kT_sb, v_sb, None, bg_dmas=bgl[:nh])
                ph.emit()
            with ExitStack() as e2:
                ph = Phase(nc, e2, "a20_", sem_es=es)
                emit_layer_a2(ph, "retnet", x_in, qT0, kT_sb, v_sb, s0, bg_dmas=bgl[nh:])
                ph.emit()
        with ExitStack() as esp2:
            php2 = Phase(nc, esp2, "p2_", sem_es=es)
            emit_prep(php2, tj0b + tj1b, [], [], src_bf16=True)
            php2.emit()
        with ExitStack() as e3:
            ph = Phase(nc, e3, "f0_", sem_es=es)
            emit_layer_f0(ph, xnT0, s0, pw["gwT"], gb, pw["w1T0"], pw["w3T0"], pw["w2T0"],
                          pw["ws1T"], pw["ws3T"], pw["ws2T"], x1)
            ph.emit()
        with ExitStack() as esl:
            kT_sb = esl.enter_context(nc.sbuf_tensor("kT1", [128, DC * T], BF16))
            v_sb = esl.enter_context(nc.sbuf_tensor("v1s", [128, NT * D], BF16))
            with ExitStack() as e4:
                ph = Phase(nc, e4, "a11_", sem_es=es)
                emit_layer_a1(ph, x1, pw["q1T"], pw["k1T"], pw["v1T"],
                              xnT1, qT1, kT_sb, v_sb, (bq, bk, bv))
                ph.emit()
            with ExitStack() as e5:
                ph = Phase(nc, e5, "a21_", sem_es=es)
                emit_layer_a2(ph, "softmax", x1, qT1, kT_sb, v_sb, s1)
                ph.emit()
        with ExitStack() as e6:
            ph = Phase(nc, e6, "f1_", sem_es=es)
            emit_layer_f1(ph, xnT1, s1, pw["r1T"], pw["w1T1"], pw["w2T1"], out)
            ph.emit()
    return nc


def kernel_fused(**inputs):
    x = np.asarray(inputs["L0_attn_x"])
    shards = [np.ascontiguousarray(x[i]) for i in range(NCORES)]
    base = {n: np.ascontiguousarray(np.asarray(inputs[n])) for n, _ in PREP_INS}
    for n in ("L0_ffn_gate_bias", "L1_attn_bq", "L1_attn_bk", "L1_attn_bv"):
        base[n] = np.ascontiguousarray(np.asarray(inputs[n]))
    nc = _prog("full", build_full_program)
    in_maps = [dict(base, x=shards[i]) for i in range(NCORES)]
    res = run_bass_kernel_spmd(nc, in_maps, core_ids=list(range(NCORES)))
    return np.stack([res.results[i]["out"] for i in range(NCORES)]).astype(np.float32)


# version 12
# speedup vs baseline: 1.1028x; 1.1028x over previous
import numpy as np
from contextlib import ExitStack
import concourse.bass as bass
import concourse.mybir as mybir
from concourse.bass_utils import run_bass_kernel_spmd

F32 = mybir.dt.float32
BF16 = mybir.dt.bfloat16
AF = mybir.ActivationFunctionType
ALU = mybir.AluOpType
AX = mybir.AxisListType

NCORES = 8
T = 4096
D = 1024
DC = D // 128
NT = T // 128
GAMMA = 0.96875
LN_EPS = 1e-5
WIN = 26
I0 = 768
E0 = 16
H1 = 4096
E1 = 8


class Sem:
    def __init__(self, h):
        self.h = h
        self.v = 0

    def inc(self, n=1):
        self.v += n
        return self.v


class Phase:
    ENGS = ("tensor", "vector", "scalar", "sync", "gpsimd")

    def __init__(self, nc, es, tag="", sem_es=None):
        self.nc = nc
        self.es = es
        self.sem_es = sem_es if sem_es is not None else es
        self.tag = tag
        self.streams = {k: [] for k in self.ENGS}
        self._semid = 0

    def sem(self, name):
        self._semid += 1
        return Sem(self.sem_es.enter_context(self.nc.semaphore(f"{self.tag}{name}{self._semid}")))

    def esem(self, eng):
        if not hasattr(self, "_esems"):
            self._esems = {}
        if eng not in self._esems:
            self._esems[eng] = self.sem(f"E{eng[:2]}")
        return self._esems[eng]

    def sb(self, name, shape, dt=F32):
        return self.es.enter_context(self.nc.sbuf_tensor(f"{self.tag}{name}", shape, dt))

    def ps(self, name, shape, dt=F32):
        return self.es.enter_context(self.nc.psum_tensor(f"{self.tag}{name}", shape, dt))

    def on(self, eng, fn):
        self.streams[eng].append(fn)

    def wait(self, eng, sem, tgt):
        if tgt is None or tgt <= 0:
            return
        self.streams[eng].append(lambda e, s=sem.h, t=tgt: e.wait_ge(s, t))

    def dma(self, q, out, in_, sem):
        tgt = sem.inc(16)
        self.streams[q].append(
            lambda e, o=out, i=in_, s=sem.h: e.dma_start(out=o, in_=i).then_inc(s, 16))
        return tgt

    def op(self, eng, fn, sem=None):
        if eng in ("vector", "scalar", "gpsimd"):
            cs = self.esem(eng)
            assert sem is None or sem is cs, f"{eng} ops must use esem"
            if cs.v > 0:
                self.streams[eng].append(lambda e, s=cs.h, t=cs.v: e.wait_ge(s, t))
            tgt = cs.inc(1)
            self.streams[eng].append(lambda e, f=fn, s=cs.h: f(e).then_inc(s, 1))
            return tgt
        if sem is None:
            self.streams[eng].append(lambda e, f=fn: f(e))
            return None
        tgt = sem.inc(1)
        self.streams[eng].append(lambda e, f=fn, s=sem.h: f(e).then_inc(s, 1))
        return tgt

    def emit(self):
        with self.nc.Block(self.tag or None) as blk:
            for name in self.ENGS:
                fns = self.streams[name]
                if not fns:
                    continue

                def body(eng, fns=fns):
                    for f in fns:
                        f(eng)
                getattr(blk, name)(body)



def emit_prep(ph, transpose_jobs, cast_jobs, small_jobs, src_bf16=False):
    nc = ph.nc
    castbuf = [ph.sb(f"pcast{i}", [128, 8 * 1024], BF16) for i in range(2)]
    stage = [ph.sb(f"pstage{i}", [128, 8 * 1024], BF16) for i in range(2)]
    castf32 = None if src_bf16 else [ph.sb(f"pcastf{i}", [128, 8 * 1024], F32) for i in range(2)]
    buf16 = ph.sb("pbuf16", [16, 1024], BF16)
    identb = ph.sb("pident", [128, 128], BF16)
    onesf = ph.sb("pones", [128, 128], F32)
    pb = [ph.ps(f"ppsum{i}", [128, 1024], F32) for i in range(2)]

    ld = [ph.sem("ld0"), ph.sem("ld1"), ph.sem("lds")]
    lcf = [ph.sem("lcf0"), ph.sem("lcf1")]
    tr = ph.sem("tr")
    ev = ph.esem("vector")
    sa = ph.esem("scalar")
    stg = [ph.sem("stg0"), ph.sem("stg1")]
    stc = [ph.sem("stc0"), ph.sem("stc1")]
    pre = ph.esem("gpsimd")

    t_on = ph.op("gpsimd", lambda e: e.memset(onesf[:], 1.0), pre)
    ph.wait("gpsimd", pre, t_on)
    t_pre = ph.op("gpsimd", lambda e: e.affine_select(
        identb[:], onesf[:], [[-1, 128]], ALU.is_equal, 0.0, channel_multiplier=1), pre)
    ph.wait("tensor", pre, t_pre)

    gi = 0
    castbuf_guard = {}
    stage_guard = {}
    bank_guard = {}
    bank_i = 0

    castf32_guard = {}

    def wait2(eng, guard):
        if guard is not None:
            ph.wait(eng, guard[0], guard[1])

    def do_group(src_grp, nstrips, CB, dst_store_out, is_small=False, rsmall=0):
        C = CB * 128
        """One transpose group: <=8 strips of 128 rows (or one small r-row strip).
        src_grp: DRAM AP arranged to match the cast-DMA target layout.
        dst_store_out: DRAM AP for the single store, matching the stage view.
        """
        nonlocal gi, bank_i
        par = gi % 2
        q = (rsmall if is_small else nstrips * 128)
        if is_small:
            wait2("gpsimd", castbuf_guard.get("small"))
            t_ld = ph.dma("gpsimd", buf16[0:rsmall, 0:CB * 128], src_grp, ld[2])
            ph.wait("tensor", ld[2], t_ld)
        elif gi % 2 == 0:
            wait2("gpsimd", castbuf_guard.get(par))
            view = castbuf[par][:, 0:nstrips * C].rearrange("p (s c) -> p s c", s=nstrips)
            t_ld = ph.dma("gpsimd", view, src_grp, ld[par])
            ph.wait("tensor", ld[par], t_ld)
        elif src_bf16:
            wait2("sync", castbuf_guard.get(par))
            view = castbuf[par][:, 0:nstrips * C].rearrange("p (s c) -> p s c", s=nstrips)
            t_ld = ph.dma("sync", view, src_grp, lcf[par])
            ph.wait("tensor", lcf[par], t_ld)
        else:
            if castf32_guard.get(par) is not None:
                ph.wait("sync", sa, castf32_guard[par])
            viewf = castf32[par][:, 0:nstrips * C].rearrange("p (s c) -> p s c", s=nstrips)
            t_lf = ph.dma("sync", viewf, src_grp, lcf[par])
            wait2("scalar", castbuf_guard.get(par))
            ph.wait("scalar", lcf[par], t_lf)
            t_cast = ph.op("scalar", lambda e, par=par, n=nstrips * C: e.activation(
                castbuf[par][:, 0:n], castf32[par][:, 0:n], AF.Copy), sa)
            castf32_guard[par] = t_cast
            ph.wait("tensor", sa, t_cast)
        if stage_guard.get(par) is not None:
            ph.wait("vector", stg[par], stage_guard[par])
        ev_end = None
        for c in range(CB):
            bpar = bank_i % 2
            bank_i += 1
            ph.wait("tensor", ev, bank_guard.get(bpar))
            for s in range(nstrips):
                if is_small:
                    fn = lambda e, c=c, s=s, bpar=bpar, r=rsmall: e.matmul(
                        pb[bpar][:, s * r:(s + 1) * r],
                        buf16[0:r, c * 128:(c + 1) * 128], identb[0:r, 0:r],
                        start=True, stop=True)
                else:
                    fn = lambda e, c=c, s=s, bpar=bpar, C=C: e.matmul(
                        pb[bpar][:, s * 128:(s + 1) * 128],
                        castbuf[par][:, s * C + c * 128: s * C + (c + 1) * 128],
                        identb[:], start=True, stop=True)
                if s == nstrips - 1:
                    t_tr = ph.op("tensor", fn, tr)
                else:
                    ph.op("tensor", fn)
            ph.wait("vector", tr, t_tr)
            ev_end = ph.op("vector", lambda e, c=c, bpar=bpar, q=q, par=par: e.tensor_copy(
                stage[par][:, c * q:(c + 1) * q], pb[bpar][:, 0:q]), ev)
            bank_guard[bpar] = ev_end
        castbuf_guard["small" if is_small else par] = (tr, tr.v)
        ph.wait("scalar", ev, ev_end)
        stv = stage[par][:, 0:CB * q].rearrange("p (c q) -> p c q", c=CB)
        t_st = ph.dma("scalar", dst_store_out, stv, stg[par])
        stage_guard[par] = t_st
        gi += 1

    for (src, dst) in transpose_jobs:
        R, C = src.shape
        CB = C // 128
        RT = R // 128
        s0 = 0
        while s0 < RT:
            ns = min(8, RT - s0, 8192 // C)
            src_grp = src[s0 * 128:(s0 + ns) * 128, :].rearrange("(s p) c -> p s c", p=128)
            dst_out = dst.rearrange("(c p) r -> p c r", p=128)[:, :, s0 * 128: s0 * 128 + ns * 128]
            do_group(src_grp, ns, CB, dst_out)
            s0 += ns

    for (src, dst) in small_jobs:
        r, C = src.shape
        CB = C // 128
        dst_out = dst.rearrange("(c p) r -> p c r", p=128)
        do_group(src, 1, CB, dst_out, is_small=True, rsmall=r)

    for (src, dst) in cast_jobs:
        R, C = src.shape
        RT = R // 128
        s0 = 0
        while s0 < RT:
            ns = min(8 * 1024 // C, RT - s0)
            par = gi % 2
            wait2("gpsimd", castbuf_guard.get(par))
            view = castbuf[par][:, 0:ns * C].rearrange("p (s c) -> p s c", s=ns)
            src_grp = src[s0 * 128:(s0 + ns) * 128, :].rearrange("(s p) c -> p s c", p=128)
            t_ld = ph.dma("gpsimd", view, src_grp, ld[par])
            ph.wait("sync", ld[par], t_ld)
            dst_out = dst[s0 * 128:(s0 + ns) * 128, :].rearrange("(s p) c -> p s c", p=128)
            t_st = ph.dma("sync", dst_out, view, stc[par])
            castbuf_guard[par] = (stc[par], t_st)
            gi += 1
            s0 += ns

    for s_ in (stg[0], stg[1], stc[0], stc[1]):
        ph.wait("sync", s_, s_.v)
        ph.wait("scalar", s_, s_.v)


PREP_OUTS = [
    ("wq0b", [D, D]), ("wk0b", [D, D]), ("wv0b", [D, D]),
    ("gwT", [D, E0]),
    ("w1T0", [E0, D, I0]), ("w3T0", [E0, D, I0]), ("w2T0", [E0, I0, D]),
    ("ws1T", [D, I0]), ("ws3T", [D, I0]), ("ws2T", [I0, D]),
    ("q1T", [D, D]), ("k1T", [D, D]), ("v1T", [D, D]),
    ("r1T", [D, E1]),
    ("w1T1", [E1, D, H1]), ("w2T1", [E1, H1, D]),
]

PREP_INS = [
    ("L0_attn_weight_q", [D, D]), ("L0_attn_weight_k", [D, D]), ("L0_attn_weight_v", [D, D]),
    ("L0_ffn_gate_w", [E0, D]),
    ("L0_ffn_w1", [E0, I0, D]), ("L0_ffn_w3", [E0, I0, D]), ("L0_ffn_w2", [E0, D, I0]),
    ("L0_ffn_ws1", [I0, D]), ("L0_ffn_ws3", [I0, D]), ("L0_ffn_ws2", [D, I0]),
    ("L1_attn_Wq", [D, D]), ("L1_attn_Wk", [D, D]), ("L1_attn_Wv", [D, D]),
    ("L1_ffn_router_w", [E1, D]),
    ("L1_ffn_w1", [E1, H1, D]), ("L1_ffn_w2", [E1, D, H1]),
]


def prep_job_lists(src, dst):
    cj = [(src["L0_attn_weight_q"], dst["wq0b"]),
          (src["L0_attn_weight_k"], dst["wk0b"]),
          (src["L0_attn_weight_v"], dst["wv0b"])]
    tj0 = []
    for e in range(E0):
        tj0.append((src["L0_ffn_w1"][e], dst["w1T0"][e]))
        tj0.append((src["L0_ffn_w3"][e], dst["w3T0"][e]))
        tj0.append((src["L0_ffn_w2"][e], dst["w2T0"][e]))
    tj0 += [(src["L0_ffn_ws1"], dst["ws1T"]), (src["L0_ffn_ws3"], dst["ws3T"]),
            (src["L0_ffn_ws2"], dst["ws2T"])]
    tj1 = [(src["L1_attn_Wq"], dst["q1T"]), (src["L1_attn_Wk"], dst["k1T"]),
           (src["L1_attn_Wv"], dst["v1T"])]
    for e in range(E1):
        tj1.append((src["L1_ffn_w1"][e], dst["w1T1"][e]))
        tj1.append((src["L1_ffn_w2"][e], dst["w2T1"][e]))
    sj = [(src["L0_ffn_gate_w"], dst["gwT"]), (src["L1_ffn_router_w"], dst["r1T"])]
    return cj, tj0, tj1, sj


def prep_jobs(ph, src, dst):
    cj, tj0, tj1, sj = prep_job_lists(src, dst)
    emit_prep(ph, tj0 + tj1, cj, sj)


def build_prep_program():
    nc = bass.Bass("TRN2", target_bir_lowering=False, debug=False)
    src = {n: nc.dram_tensor(n, s, F32, kind="ExternalInput").ap() for n, s in PREP_INS}
    dst = {n: nc.dram_tensor(n, s, BF16, kind="ExternalOutput").ap() for n, s in PREP_OUTS}
    es = ExitStack()
    with es:
        ph = Phase(nc, es, "prep_")
        prep_jobs(ph, src, dst)
        ph.emit()
    return nc



def emit_layer_a1(ph, x_in, wq, wk, wv, xnT_out, qT_out, kT_sb, v_sb, biases=None,
                  bg_dmas=None):
    nc = ph.nc
    G = T // 512
    xbuf = [ph.sb(f"a1x{i}", [128, D], F32) for i in range(2)]
    xnln = [ph.sb(f"a1xn{i}", [128, D], BF16) for i in range(4)]
    xnT_g = ph.sb("a1xnTg", [128, DC * 512], BF16)
    qT_g = ph.sb("a1qTg", [128, DC * 512], BF16)
    wbuf = [ph.sb(f"a1w{i}", [128, DC * D], BF16) for i in range(2)]
    identb = ph.sb("a1ident", [128, 128], BF16)
    onesf = ph.sb("a1ones", [128, 128], F32)
    r1 = ph.sb("a1r1", [128, 1], F32)
    r2 = ph.sb("a1r2", [128, 1], F32)
    mcol = ph.sb("a1m", [128, 1], F32)
    vcol = ph.sb("a1v", [128, 1], F32)
    m2col = ph.sb("a1m2", [128, 1], F32)
    rstd = ph.sb("a1rstd", [128, 1], F32)
    has_bias = biases is not None
    if has_bias:
        bq_sb = ph.sb("a1bq", [128, DC], F32)
        bk_sb = ph.sb("a1bk", [128, DC], F32)
        bv_sb = ph.sb("a1bvc", [128, DC], F32)
        bvb = ph.sb("a1bvb", [128, D], BF16)
    ps_xT = ph.ps("a1psxT", [128, D], F32)
    ps_sq = ph.ps("a1pssq", [128, D], F32)
    ps_mm = [ph.ps(f"a1psmm{i}", [128, 512], F32) for i in range(4)]

    ldx = [ph.sem("ldx0"), ph.sem("ldx1")]
    ldb = ph.sem("ldb")
    wl = [ph.sem("wl0"), ph.sem("wl1")]
    soxn = ph.sem("soxn")
    soq = ph.sem("soq")
    sp = ph.sem("sp")
    sv = ph.esem("vector")
    sa = ph.esem("scalar")
    sg = ph.esem("gpsimd")

    t_on = ph.op("gpsimd", lambda e: e.memset(onesf[:], 1.0), sg)
    ph.wait("gpsimd", sg, t_on)
    t_id = ph.op("gpsimd", lambda e: e.affine_select(
        identb[:], onesf[:], [[-1, 128]], ALU.is_equal, 0.0, channel_multiplier=1), sg)
    ph.wait("tensor", sg, t_id)
    if has_bias:
        bq, bk, bv = biases
        for c in range(DC):
            ph.dma("sync", bq_sb[:, c:c + 1], bq[c * 128:(c + 1) * 128].unsqueeze(1), ldb)
            ph.dma("sync", bk_sb[:, c:c + 1], bk[c * 128:(c + 1) * 128].unsqueeze(1), ldb)
            ph.dma("sync", bv_sb[:, c:c + 1], bv[c * 128:(c + 1) * 128].unsqueeze(1), ldb)
        bvrow = ph.sb("a1bvrow", [1, D], F32)
        ph.dma("sync", bvrow[:], bv.unsqueeze(0), ldb)
        ph.wait("tensor", ldb, ldb.v)
        ph.wait("scalar", ldb, ldb.v)
        for n in range(2):
            t_bb = ph.op("tensor", lambda e, n=n: e.matmul(
                ps_sq[:, n * 512:(n + 1) * 512], onesf[0:1, :], bvrow[:, n * 512:(n + 1) * 512],
                start=True, stop=True), sp)
        ph.wait("vector", sp, t_bb)
        t_bv = ph.op("vector", lambda e: e.tensor_copy(bvb[:], ps_sq[:]), sv)
        ph.wait("scalar", sv, t_bv)

    wbuf_guard = {}
    psum_guard = {}
    W_APS = [wq, wk, wv]

    def load_w(gi_w, par):
        src = W_APS[gi_w % 3]
        g = wbuf_guard.get(par)
        if g is not None:
            ph.wait("sync", sp, g)
        return ph.dma("sync", wbuf[par][:].rearrange("p (c d) -> p c d", c=DC),
                      src.rearrange("(c p) d -> p c d", p=128), wl[par])

    wq_t = load_w(0, 0)
    wk_t = load_w(1, 1)
    wloads = [wq_t, wk_t]
    wi = 2
    bgsem = ph.sem("bg")
    bg_dmas = list(bg_dmas or [])
    bg_i = 0

    def pump_bg(n):
        nonlocal bg_i
        for _ in range(n):
            if bg_i < len(bg_dmas):
                if sp.v > 0:
                    ph.wait("gpsimd", sp, sp.v)
                dst_, src_ = bg_dmas[bg_i]
                ph.dma("gpsimd", dst_, src_, bgsem)
                bg_i += 1

    t_xn_done = {}
    qkv_done_prev = None
    st_xn_prev = None
    st_q_prev = None
    xbuf_guard = {}
    xnln_guard = {}

    for g in range(G):
        t_xns = {}
        for tau in range(4):
            i = g * 4 + tau
            par = i % 2
            if xbuf_guard.get(par) is not None:
                ph.wait("sync", sv, xbuf_guard[par])
            t_x = ph.dma("sync", xbuf[par][:], x_in[i * 128:(i + 1) * 128, :], ldx[par])
            ph.wait("vector", ldx[par], t_x)
            t_r1 = ph.op("vector", lambda e, par=par: e.reduce_sum(r1[:], xbuf[par][:], axis=AX.X), sv)
            ph.wait("scalar", ldx[par], t_x)
            ph.wait("scalar", sv, t_r1)
            t_sq = ph.op("scalar", lambda e, par=par: e.activation(
                ps_sq[:], xbuf[par][:], AF.Square, accum_out=r2[:]), sa)
            ph.wait("vector", sa, t_sq)
            ph.op("vector", lambda e: e.tensor_scalar(mcol[:], r1[:], 1.0 / D, None, ALU.mult))
            ph.op("vector", lambda e: e.tensor_scalar(vcol[:], r2[:], 1.0 / D, float(LN_EPS),
                                                      ALU.mult, ALU.add))
            ph.op("vector", lambda e: e.tensor_tensor(m2col[:], mcol[:], mcol[:], ALU.mult))
            t_var = ph.op("vector", lambda e: e.tensor_tensor(vcol[:], vcol[:], m2col[:], ALU.subtract), sv)
            ph.wait("scalar", sv, t_var)
            ph.op("scalar", lambda e: e.activation(vcol[:], vcol[:], AF.Ln))
            t_rstd = ph.op("scalar", lambda e: e.activation(rstd[:], vcol[:], AF.Exp, scale=-0.5), sa)
            ph.wait("vector", sa, t_rstd)
            if xnln_guard.get(tau) is not None:
                ph.wait("vector", sp, xnln_guard[tau])
            t_xn = ph.op("vector", lambda e, par=par, tau=tau: e.tensor_scalar(
                xnln[tau][:], xbuf[par][:], mcol[:], rstd[:], ALU.subtract, ALU.mult), sv)
            t_xns[tau] = t_xn
            xbuf_guard[par] = t_xn
        for tau in range(4):
            i = g * 4 + tau
            ph.wait("tensor", sv, t_xns[tau])
            if tau == 0 and qkv_done_prev is not None:
                ph.wait("vector", sp, qkv_done_prev)
                ph.wait("vector", soxn, st_xn_prev)
            for c in range(DC):
                fn = lambda e, c=c, tau=tau: e.matmul(
                    ps_xT[:, c * 128:(c + 1) * 128], xnln[tau][:, c * 128:(c + 1) * 128],
                    identb[:], start=True, stop=True)
                if c == DC - 1:
                    t_tr = ph.op("tensor", fn, sp)
                else:
                    ph.op("tensor", fn)
            xnln_guard[tau] = t_tr
            ph.wait("vector", sp, t_tr)
            t_ev = ph.op("vector", lambda e, tau=tau: e.tensor_copy(
                xnT_g[:].rearrange("p (c t) -> p c t", c=DC)[:, :, tau * 128:(tau + 1) * 128],
                ps_xT[:].rearrange("p (c t) -> p c t", c=DC)), sv)
            t_xn_done[i] = t_ev
            ph.wait("tensor", sv, t_ev)

        ph.wait("sync", sv, t_xn_done[g * 4 + 3])
        t_st_xn = ph.dma("sync", xnT_out.rearrange("(c p) t -> p c t", p=128)[:, :, g * 512:(g + 1) * 512],
                         xnT_g[:].rearrange("p (c t) -> p c t", c=DC), soxn)
        st_xn_prev = t_st_xn

        grp_ready = t_xn_done[g * 4 + 3]
        ph.wait("tensor", sv, grp_ready)
        for proj in range(3):
            wpar = (g * 3 + proj) % 2
            ph.wait("tensor", wl[wpar], wloads[g * 3 + proj])
            if proj < 2:
                for m in range(DC):
                    pidx = m % 4
                    pm = ps_mm[pidx]
                    if psum_guard.get(pidx) is not None:
                        gs, gt = psum_guard[pidx]
                        ph.wait("tensor", gs, gt)
                    for c in range(DC):
                        fn = lambda e, m=m, c=c, pm=pm, wpar=wpar: e.matmul(
                            pm[:], wbuf[wpar][:, c * D + m * 128: c * D + (m + 1) * 128],
                            xnT_g[:, c * 512:(c + 1) * 512],
                            start=(c == 0), stop=(c == DC - 1))
                        if c == DC - 1:
                            t_mm = ph.op("tensor", fn, sp)
                        else:
                            ph.op("tensor", fn)
                    ph.wait("scalar", sp, t_mm)
                    if proj == 0:
                        if st_q_prev is not None and m == 0:
                            ph.wait("scalar", soq, st_q_prev)
                        dst = qT_g[:, m * 512:(m + 1) * 512]
                        bias_ap = bq_sb[:, m:m + 1] if has_bias else None
                    else:
                        dst = kT_sb[:, m * T + g * 512: m * T + (g + 1) * 512]
                        bias_ap = bk_sb[:, m:m + 1] if has_bias else None
                    if bias_ap is not None:
                        t_ev2 = ph.op("scalar", lambda e, dst=dst, pm=pm, b=bias_ap: e.activation(
                            dst, pm[:], AF.Identity, bias=b), sa)
                    else:
                        t_ev2 = ph.op("scalar", lambda e, dst=dst, pm=pm: e.activation(
                            dst, pm[:], AF.Copy), sa)
                    psum_guard[pidx] = (sa, t_ev2)
            else:
                for tau in range(4):
                    i = g * 4 + tau
                    for n in range(2):
                        pidx = 2 * (tau % 2) + n
                        pm = ps_mm[pidx]
                        if psum_guard.get(pidx) is not None:
                            gs, gt = psum_guard[pidx]
                            ph.wait("tensor", gs, gt)
                        for c in range(DC):
                            fn = lambda e, tau=tau, n=n, c=c, pm=pm, wpar=wpar: e.matmul(
                                pm[:], xnT_g[:, c * 512 + tau * 128: c * 512 + (tau + 1) * 128],
                                wbuf[wpar][:, c * D + n * 512: c * D + (n + 1) * 512],
                                start=(c == 0), stop=(c == DC - 1))
                            if c == DC - 1:
                                t_mm = ph.op("tensor", fn, sp)
                            else:
                                ph.op("tensor", fn)
                        dst = v_sb[:, i * D + n * 512: i * D + (n + 1) * 512]
                        if has_bias:
                            ph.wait("vector", sp, t_mm)
                            t_ev3 = ph.op("vector", lambda e, dst=dst, pm=pm, n=n: e.tensor_tensor(
                                dst, pm[:], bvb[:, n * 512:(n + 1) * 512], ALU.add), sv)
                            psum_guard[pidx] = (sv, t_ev3)
                        else:
                            ph.wait("scalar", sp, t_mm)
                            t_ev2 = ph.op("scalar", lambda e, dst=dst, pm=pm: e.activation(
                                dst, pm[:], AF.Copy), sa)
                            psum_guard[pidx] = (sa, t_ev2)
            wbuf_guard[wpar] = sp.v
            if wi < 3 * G:
                wloads.append(load_w(wi, wi % 2))
                wi += 1
        qkv_done_prev = sp.v
        pump_bg((len(bg_dmas) + G - 1) // G if bg_dmas else 0)
        last_q_evac = sa.v
        ph.wait("sync", sa, last_q_evac)
        st_q_prev = ph.dma("sync", qT_out.rearrange("(c p) t -> p c t", p=128)[:, :, g * 512:(g + 1) * 512],
                           qT_g[:].rearrange("p (c t) -> p c t", c=DC), soq)

    pump_bg(len(bg_dmas))
    ph.wait("gpsimd", bgsem, bgsem.v)
    ph.wait("sync", soxn, soxn.v)
    ph.wait("sync", soq, soq.v)
    ph.wait("sync", sa, sa.v)
    ph.wait("sync", sv, sv.v)



def emit_layer_a2(ph, mode, x_in, qT_in, kT_sb, v_sb, s_out, bg_dmas=None):
    nc = ph.nc
    retnet = (mode == "retnet")
    qtile = [ph.sb(f"a2q{i}", [128, DC * 128], BF16) for i in range(2)]
    xbuf = [ph.sb(f"a2x{i}", [128, D], F32) for i in range(2)]
    sbuf = [ph.sb(f"a2s{i}", [128, D], F32) for i in range(2)]
    probsT = [ph.sb(f"a2pT{i}", [128, 512], BF16) for i in range(2)]
    identb = ph.sb("a2ident", [128, 128], BF16)
    onesf = ph.sb("a2ones", [128, 128], F32)
    if retnet:
        probs = [ph.sb(f"a2probs{i}", [128, WIN * 128], BF16) for i in range(2)]
        Mdec = ph.sb("a2M", [128, (WIN + 1) * 128], F32)
        if32 = ph.sb("a2iota", [128, 128], F32)
        mtmp = ph.sb("a2mtmp", [128, 128], F32)
    else:
        probs = [ph.sb(f"a2probs{i}", [128, T], BF16) for i in range(2)]
        scores = [ph.sb(f"a2scores{i}", [128, T], BF16) for i in range(2)]
        tri = ph.sb("a2tri", [128, 128], BF16)
        rmax = [ph.sb(f"a2rmax{k}", [128, 1], F32) for k in range(2)]
        gmax = [ph.sb(f"a2gmax{k}", [128, 1], F32) for k in range(2)]
        nm32 = [ph.sb(f"a2nm32{k}", [128, 1], F32) for k in range(2)]
        gsum = [ph.sb(f"a2gsum{k}", [128, 12], F32) for k in range(2)]
        rsum = [ph.sb(f"a2rsum{k}", [128, 1], F32) for k in range(2)]
        rcol = [ph.sb(f"a2rcol{k}", [128, 1], F32) for k in range(2)]
    ps_sc = [ph.ps(f"a2pssc{i}", [128, 512], F32) for i in range(2)]
    ps_pt = [ph.ps(f"a2pspt{i}", [128, 512], F32) for i in range(2)]
    ps_out = [ph.ps(f"a2psout{i}", [128, D], F32) for i in range(2)]

    ldq = [ph.sem("ldq0"), ph.sem("ldq1")]
    ldxx = [ph.sem("ldxx0"), ph.sem("ldxx1")]
    sos = [ph.sem("sos0"), ph.sem("sos1")]
    sp = ph.sem("sp")
    sv = ph.esem("vector")
    sa = ph.esem("scalar")
    sg = ph.esem("gpsimd")

    t_on = ph.op("gpsimd", lambda e: e.memset(onesf[:], 1.0), sg)
    ph.wait("gpsimd", sg, t_on)
    t_id = ph.op("gpsimd", lambda e: e.affine_select(
        identb[:], onesf[:], [[-1, 128]], ALU.is_equal, 0.0, channel_multiplier=1), sg)
    ph.wait("tensor", sg, t_id)
    lgam = float(np.log(GAMMA))
    if retnet:
        t_io = ph.op("gpsimd", lambda e: e.iota(
            if32[:], [[-1, 128]], channel_multiplier=1,
            allow_small_or_imprecise_dtypes=True), sg)
        ph.wait("scalar", sg, t_io)
        t_m0 = ph.op("scalar", lambda e: e.activation(mtmp[:], if32[:], AF.Exp, scale=lgam), sa)
        ph.wait("vector", sa, t_m0)
        for d in range(1, WIN):
            gd = float(GAMMA ** (128 * d))
            t_md = ph.op("vector", lambda e, d=d, gd=gd: e.tensor_scalar(
                Mdec[:, d * 128:(d + 1) * 128], mtmp[:], gd, None, ALU.mult), sv)
        ph.wait("gpsimd", sa, t_m0)
        t_msk = ph.op("gpsimd", lambda e: e.affine_select(
            Mdec[:, 0:128], mtmp[:], [[-1, 128]], ALU.is_ge, 0.0,
            channel_multiplier=1), sg)
        ph.wait("vector", sg, t_msk)
    else:
        t_tri = ph.op("gpsimd", lambda e: e.affine_select(
            tri[:], onesf[:], [[-1, 128]], ALU.is_ge, 0.0, channel_multiplier=1), sg)
        ph.wait("vector", sg, t_tri)
        ph.wait("scalar", sg, t_tri)

    psc_guard = {}
    ppt_guard = {}
    probs_guard = {}
    scores_guard = {}
    sbuf_guard = {}
    psout_guard = {}
    last_dg = {}
    state = {}


    def part1(i):
        par = i % 2
        if sbuf_guard.get(("q", par)) is not None:
            gs, gt = sbuf_guard[("q", par)]
            ph.wait("sync", gs, gt)
        t_q = ph.dma("sync", qtile[par][:],
                     qT_in.rearrange("(c p) t -> p c t", p=128)[:, :, i * 128:(i + 1) * 128], ldq[par])
        t_x = ph.dma("sync", xbuf[par][:], x_in[i * 128:(i + 1) * 128, :], ldxx[par])
        ph.wait("tensor", ldq[par], t_q)

        j0 = max(0, i - (WIN - 1)) if retnet else 0
        nk = (i + 1 - j0) * 128
        k0 = j0 * 128
        nspan = (nk + 511) // 512
        st = dict(t_x=t_x, j0=j0, nk=nk)

        pbuf = probs[par]
        if retnet and probs_guard.get(par) is not None:
            ph.wait("vector", sp, probs_guard[par])
        if not retnet:
            sbuf_sc = scores[par]
            if scores_guard.get(par) is not None:
                ph.wait("vector", sa, scores_guard[par])
        for s0 in range(nspan):
            ks = k0 + s0 * 512
            w = min(512, k0 + nk - ks)
            pidx = s0 % 2
            pm = ps_sc[pidx]
            if psc_guard.get(pidx) is not None:
                gs, gt = psc_guard[pidx]
                ph.wait("tensor", gs, gt)
            for c in range(DC):
                fn = lambda e, c=c, pm=pm, par=par, ks=ks, w=w: e.matmul(
                    pm[:, 0:w], qtile[par][:, c * 128:(c + 1) * 128],
                    kT_sb[:, c * T + ks: c * T + ks + w],
                    start=(c == 0), stop=(c == DC - 1))
                if c == DC - 1:
                    t_mm = ph.op("tensor", fn, sp)
                else:
                    ph.op("tensor", fn)
            ph.wait("vector", sp, t_mm)
            if retnet:
                nb = w // 128
                jst = ks // 128
                d0 = i - jst
                t_e = ph.op("vector", lambda e, nb=nb, jst=jst, j0=j0, d0=d0, pidx=pidx, pbuf=pbuf: e.tensor_tensor(
                    pbuf[:, (jst - j0) * 128:(jst - j0 + nb) * 128].rearrange("p (b q) -> p b q", b=nb),
                    ps_sc[pidx][:, 0:nb * 128].rearrange("p (b q) -> p b q", b=nb),
                    bass.AP(Mdec, d0 * 128, [[(WIN + 1) * 128, 128], [-128, nb], [1, 128]]),
                    ALU.mult), sv)
            else:
                ph.wait("scalar", sp, t_mm)
                t_e = ph.op("scalar", lambda e, ks=ks, w=w, pidx=pidx, sbuf_sc=sbuf_sc: e.activation(
                    sbuf_sc[:, ks:ks + w], ps_sc[pidx][:, 0:w], AF.Copy), sa)
            psc_guard[pidx] = (sv, t_e) if retnet else (sa, t_e)
        if retnet:
            st["t_probs_done"] = sv.v
        else:
            ph.wait("vector", sa, t_e)
            ph.op("vector", lambda e, par=par, nk=nk, sbuf_sc=sbuf_sc: e.reduce_max(
                rmax[par][:], sbuf_sc[:, 0:nk], axis=AX.X))
            if last_dg.get(par) is not None:
                ph.wait("vector", sa, last_dg[par])
            t_nm = ph.op("vector", lambda e, par=par: e.tensor_scalar(
                nm32[par][:], rmax[par][:], -1.0 / 32.0, None, ALU.mult), sv)
            ph.wait("scalar", sv, t_nm)
            if probs_guard.get(par) is not None:
                ph.wait("scalar", sp, probs_guard[par])
            ns = 0
            for s0 in range(nspan):
                ks = s0 * 512
                w = min(512, nk - ks)
                wf = w if ks + w <= i * 128 else max(0, i * 128 - ks)
                if wf > 0:
                    ph.op("scalar", lambda e, ks=ks, wf=wf, ns=ns, pbuf=pbuf, sbuf_sc=sbuf_sc, par=par: e.activation(
                        pbuf[:, ks:ks + wf], sbuf_sc[:, ks:ks + wf], AF.Exp,
                        scale=1.0 / 32.0, bias=nm32[par][:], accum_out=gsum[par][:, ns:ns + 1]))
                    ns += 1
            t_dg = ph.op("scalar", lambda e, i=i, pbuf=pbuf, sbuf_sc=sbuf_sc, par=par: e.activation(
                pbuf[:, i * 128:(i + 1) * 128], sbuf_sc[:, i * 128:(i + 1) * 128], AF.Exp,
                scale=1.0 / 32.0, bias=nm32[par][:]), sa)
            scores_guard[par] = t_dg
            last_dg[par] = t_dg
            ph.wait("vector", sa, t_dg)
            t_dm = ph.op("vector", lambda e, i=i, pbuf=pbuf: e.tensor_tensor(
                pbuf[:, i * 128:(i + 1) * 128], pbuf[:, i * 128:(i + 1) * 128],
                tri[:], ALU.mult), sv)
            ph.op("vector", lambda e, i=i, ns=ns, pbuf=pbuf, par=par: e.reduce_sum(
                gsum[par][:, ns:ns + 1], pbuf[:, i * 128:(i + 1) * 128], axis=AX.X))
            ph.op("vector", lambda e, ns=ns, par=par: e.reduce_sum(
                rsum[par][:], gsum[par][:, 0:ns + 1], axis=AX.X))
            t_rc = ph.op("vector", lambda e, par=par: e.reciprocal(rcol[par][:], rsum[par][:]), sv)
            st["t_dm"] = t_dm
            st["t_dg"] = t_dg
            st["t_rc"] = t_rc
        state[i] = st

    def part2(i):
        par = i % 2
        st = state.pop(i)
        j0, nk, t_x = st["j0"], st["nk"], st["t_x"]
        pbuf = probs[par]
        pso = ps_out[par]
        if psout_guard.get(par) is not None:
            gs, gt = psout_guard[par]
            ph.wait("tensor", gs, gt)
        first = True
        nblk = nk // 128
        tgt_out = None
        for b0 in range(0, nblk, 4):
            bn = min(4, nblk - b0)
            tpidx = (b0 // 4) % 2
            if ppt_guard.get(("ps", tpidx)) is not None:
                ph.wait("tensor", sv, ppt_guard[("ps", tpidx)])
            if b0 == 0:
                if retnet:
                    ph.wait("tensor", sv, st["t_probs_done"])
                else:
                    ph.wait("tensor", sv, st["t_dm"])
                    ph.wait("tensor", sa, st["t_dg"])
            for b in range(bn):
                fn = lambda e, b0=b0, b=b, tpidx=tpidx, pbuf=pbuf: e.matmul(
                    ps_pt[tpidx][:, b * 128:(b + 1) * 128],
                    pbuf[:, (b0 + b) * 128:(b0 + b + 1) * 128], identb[:],
                    start=True, stop=True)
                if b == bn - 1:
                    t_tp = ph.op("tensor", fn, sp)
                else:
                    ph.op("tensor", fn)
            if ppt_guard.get(("sb", tpidx)) is not None:
                ph.wait("vector", sp, ppt_guard[("sb", tpidx)])
            ph.wait("vector", sp, t_tp)
            t_pev = ph.op("vector", lambda e, tpidx=tpidx, bn=bn: e.tensor_copy(
                probsT[tpidx][:, 0:bn * 128], ps_pt[tpidx][:, 0:bn * 128]), sv)
            ppt_guard[("ps", tpidx)] = t_pev
            ph.wait("tensor", sv, t_pev)
            for b in range(bn):
                j = j0 + b0 + b
                final = (b0 + b == nblk - 1)
                for n in range(2):
                    last = (b == bn - 1 and n == 1)
                    fn = lambda e, b=b, j=j, n=n, tpidx=tpidx, first=first, final=final, pso=pso: e.matmul(
                        pso[:, n * 512:(n + 1) * 512],
                        probsT[tpidx][:, b * 128:(b + 1) * 128],
                        v_sb[:, j * D + n * 512: j * D + (n + 1) * 512],
                        start=first, stop=final)
                    if last:
                        tgt_out = ph.op("tensor", fn, sp)
                    else:
                        ph.op("tensor", fn)
                first = False
            ppt_guard[("sb", tpidx)] = tgt_out
        probs_guard[par] = tgt_out

        ph.wait("vector", sp, tgt_out)
        ph.wait("vector", ldxx[par], t_x)
        if sbuf_guard.get(("s", par)) is not None:
            gs, gt = sbuf_guard[("s", par)]
            ph.wait("vector", gs, gt)
        if retnet:
            fn = lambda e, par=par, pso=pso: e.tensor_tensor(
                sbuf[par][:], pso[:], xbuf[par][:], ALU.add)
        else:
            fn = lambda e, par=par, pso=pso: e.scalar_tensor_tensor(
                sbuf[par][:], pso[:], rcol[par][:], xbuf[par][:], ALU.mult, ALU.add)
        t_sev = ph.op("vector", fn, sv)
        psout_guard[par] = (sv, t_sev)
        sbuf_guard[("q", par)] = (sv, t_sev)
        ph.wait("sync", sv, t_sev)
        t_st = ph.dma("sync", s_out[i * 128:(i + 1) * 128, :], sbuf[par][:], sos[par])
        sbuf_guard[("s", par)] = (sos[par], t_st)

    bgsem = ph.sem("bg")
    bg_dmas = list(bg_dmas or [])
    bg_i = 0

    def pump_bg(n):
        nonlocal bg_i
        for _ in range(n):
            if bg_i < len(bg_dmas):
                if sp.v > 0:
                    ph.wait("gpsimd", sp, sp.v)
                dst_, src_ = bg_dmas[bg_i]
                ph.dma("gpsimd", dst_, src_, bgsem)
                bg_i += 1

    part1(0)
    for i in range(NT):
        if i + 1 < NT:
            part1(i + 1)
        part2(i)
        pump_bg((len(bg_dmas) + NT - 1) // NT if bg_dmas else 0)

    pump_bg(len(bg_dmas))
    ph.wait("gpsimd", bgsem, bgsem.v)
    ph.wait("sync", sos[0], sos[0].v)
    ph.wait("sync", sos[1], sos[1].v)


def build_layer_attn_program(layer):
    nc = bass.Bass("TRN2", target_bir_lowering=False, debug=False)
    x_in = nc.dram_tensor("x", [T, D], F32, kind="ExternalInput").ap()
    if layer == 0:
        wq = nc.dram_tensor("wq0b", [D, D], BF16, kind="ExternalInput").ap()
        wk = nc.dram_tensor("wk0b", [D, D], BF16, kind="ExternalInput").ap()
        wv = nc.dram_tensor("wv0b", [D, D], BF16, kind="ExternalInput").ap()
        biases = None
        mode = "retnet"
    else:
        wq = nc.dram_tensor("q1T", [D, D], BF16, kind="ExternalInput").ap()
        wk = nc.dram_tensor("k1T", [D, D], BF16, kind="ExternalInput").ap()
        wv = nc.dram_tensor("v1T", [D, D], BF16, kind="ExternalInput").ap()
        bq = nc.dram_tensor("bq", [D], F32, kind="ExternalInput").ap()
        bk = nc.dram_tensor("bk", [D], F32, kind="ExternalInput").ap()
        bv = nc.dram_tensor("bv", [D], F32, kind="ExternalInput").ap()
        biases = (bq, bk, bv)
        mode = "softmax"
    xnT_out = nc.dram_tensor("xnT", [D, T], BF16, kind="ExternalOutput").ap()
    qT_hbm = nc.dram_tensor("qT", [D, T], BF16, kind="Internal").ap()
    s_out = nc.dram_tensor("s", [T, D], F32, kind="ExternalOutput").ap()

    dbg = getattr(build_layer_attn_program, "debug", False)
    if dbg:
        dbg_k = nc.dram_tensor("dbg_k", [D, T], BF16, kind="ExternalOutput").ap()
        dbg_v = nc.dram_tensor("dbg_v", [T, D], BF16, kind="ExternalOutput").ap()
        dbg_q = nc.dram_tensor("dbg_q", [D, T], BF16, kind="ExternalOutput").ap()
    with ExitStack() as es:
        kT_sb = es.enter_context(nc.sbuf_tensor("kT", [128, DC * T], BF16))
        v_sb = es.enter_context(nc.sbuf_tensor("v", [128, NT * D], BF16))
        with ExitStack() as es1:
            ph1 = Phase(nc, es1, "a1_", sem_es=es)
            emit_layer_a1(ph1, x_in, wq, wk, wv, xnT_out, qT_hbm, kT_sb, v_sb, biases)
            ph1.emit()
        if dbg:
            with ExitStack() as esd:
                phd = Phase(nc, esd, "dbg_", sem_es=es)
                sod = phd.sem("sod")
                phd.dma("sync", dbg_k.rearrange("(c p) t -> p c t", p=128),
                        kT_sb[:].rearrange("p (c t) -> p c t", c=DC), sod)
                phd.dma("sync", dbg_v.rearrange("(i p) d -> p i d", p=128),
                        v_sb[:].rearrange("p (i d) -> p i d", i=NT), sod)
                phd.dma("sync", dbg_q, qT_hbm, sod)
                phd.wait("sync", sod, sod.v)
                phd.emit()
        with ExitStack() as es2:
            ph2 = Phase(nc, es2, "a2_", sem_es=es)
            emit_layer_a2(ph2, mode, x_in, qT_hbm, kT_sb, v_sb, s_out)
            ph2.emit()
    return nc



def emit_layer_f0(ph, xnT_in, s_in, gwT, gb, w1T, w3T, w2T, ws1T, ws3T, ws2T, x_out):
    nc = ph.nc
    NQ = T // 1024
    IC = I0 // 128
    xnT_q = [ph.sb(f"f0xnT{i}", [128, DC * 1024], BF16) for i in range(2)]
    x_acc = ph.sb("f0xacc", [128, DC * 1024], F32)
    wb1 = [ph.sb(f"f0w1_{i}", [128, DC * I0], BF16) for i in range(2)]
    wb3 = [ph.sb(f"f0w3_{i}", [128, DC * I0], BF16) for i in range(2)]
    wb2 = [ph.sb(f"f0w2_{i}", [128, IC * D], BF16) for i in range(2)]
    gsil = ph.sb("f0gsil", [128, IC * 1024], F32)
    hact = ph.sb("f0hact", [128, IC * 1024], BF16)
    gwT_sb = ph.sb("f0gw", [128, DC * E0], BF16)
    gbrow = ph.sb("f0gbrow", [1, E0], F32)
    onesr = ph.sb("f0ones", [1, 128], F32)
    gbb = ph.sb("f0gbb", [128, E0], F32)
    onescol = ph.sb("f0onec", [128, 1], F32)
    scq = ph.sb("f0scq", [128, 8 * E0], F32)
    comb = ph.sb("f0comb", [128, 8 * E0], F32)
    NTQ = 8
    RW = NTQ * E0
    gbb8 = ph.sb("f0gbb8", [128, RW], F32)
    sbv8 = ph.sb("f0sb8", [128, RW], F32)
    prt8 = ph.sb("f0prt8", [128, 6 * NTQ * 4], F32)
    gs8 = ph.sb("f0gs8", [128, NTQ * 4], F32)
    mn1 = ph.sb("f0mn1", [128, NTQ], F32)
    mn2 = ph.sb("f0mn2", [128, NTQ], F32)
    gm8 = ph.sb("f0gm8", [128, NTQ * 4], F32)
    em8 = ph.sb("f0em8", [128, RW], F32)
    msk8 = ph.sb("f0msk8", [128, RW], F32)
    wm8 = ph.sb("f0wm8", [128, RW], F32)
    sel8 = ph.sb("f0sel8", [128, RW], F32)
    t18 = ph.sb("f0t18", [128, RW], F32)
    mx8 = ph.sb("f0mx8", [128, NTQ], F32)
    rs8 = ph.sb("f0rs8", [128, NTQ], F32)
    rc8 = ph.sb("f0rc8", [128, NTQ], F32)

    ps_r = [ph.ps(f"f0psr{i}", [128, 512], F32) for i in range(2)]
    ps_g = [ph.ps(f"f0psg{i}", [128, 512], F32) for i in range(2)]
    ps_u = [ph.ps(f"f0psu{i}", [128, 512], F32) for i in range(2)]
    ps_y = [ph.ps(f"f0psy{i}", [128, 512], F32) for i in range(2)]

    lda = [ph.sem("lda0"), ph.sem("lda1")]
    ldw = [ph.sem("ldw0"), ph.sem("ldw1")]
    ldc = ph.sem("ldc")
    sox = ph.sem("sox")
    sp = ph.sem("sp")
    sv = ph.esem("vector")
    sa = ph.esem("scalar")
    sg = ph.esem("gpsimd")

    wload_guard = {}
    psum_guard = {}
    ph.dma("sync", gwT_sb[:].rearrange("p (c e) -> p c e", c=DC),
           gwT.rearrange("(c p) e -> p c e", p=128), ldc)
    ph.dma("sync", gbrow[:], gb.unsqueeze(0), ldc)
    t_c = ph.op("gpsimd", lambda e: e.memset(onesr[:], 1.0), sg)
    t_c2 = ph.op("gpsimd", lambda e: e.memset(onescol[:], 1.0), sg)
    ph.wait("tensor", ldc, ldc.v)
    ph.wait("tensor", sg, t_c)
    t_gb = ph.op("tensor", lambda e: e.matmul(ps_r[0][:, 0:E0], onesr[:], gbrow[:],
                                              start=True, stop=True), sp)
    ph.wait("vector", sp, t_gb)
    t_gbb = ph.op("vector", lambda e: e.tensor_copy(gbb[:], ps_r[0][:, 0:E0]), sv)
    for tt in range(8):
        t_gbb = ph.op("vector", lambda e, tt=tt: e.tensor_copy(
            gbb8[:, tt * E0:(tt + 1) * E0], gbb[:]), sv)
    psum_guard_init = (sv, t_gbb)
    ph.wait("vector", sg, t_c2)

    psum_guard[("r", 0)] = psum_guard_init
    W1S = [w1T[e] for e in range(E0)] + [ws1T]
    W3S = [w3T[e] for e in range(E0)] + [ws3T]
    W2S = [w2T[e] for e in range(E0)] + [ws2T]
    NE = E0 + 1

    acc_guard = None
    xnq_guard = {}
    hact_guard = None
    gsil_guard = None
    wloads = {}

    def load_we(ei, par):
        if wload_guard.get(par) is not None:
            ph.wait("gpsimd", sp, wload_guard[par])
        ph.dma("gpsimd", wb1[par][:].rearrange("p (c i) -> p c i", c=DC),
               W1S[ei].rearrange("(c p) i -> p c i", p=128), ldw[par])
        ph.dma("gpsimd", wb3[par][:].rearrange("p (c i) -> p c i", c=DC),
               W3S[ei].rearrange("(c p) i -> p c i", p=128), ldw[par])
        return ph.dma("gpsimd", wb2[par][:].rearrange("p (s d) -> p s d", s=IC),
                      W2S[ei].rearrange("(s p) d -> p s d", p=128), ldw[par])

    wloads[0] = load_we(0, 0)
    wloads[1] = load_we(1, 1)
    nextload = 2

    ldxn = [ph.sem("ldxn0"), ph.sem("ldxn1")]
    xn_loads = {}

    def load_xnq(Q):
        qp = Q % 2
        g = xnq_guard.get(qp)
        if g is not None:
            ph.wait("sync", sp, g)
        xn_loads[Q] = ph.dma("sync", xnT_q[qp][:].rearrange("p (c t) -> p c t", c=DC),
                             xnT_in.rearrange("(c p) t -> p c t", p=128)[:, :, Q * 1024:(Q + 1) * 1024],
                             ldxn[qp])

    load_xnq(0)
    for Q in range(NQ):
        qp = Q % 2
        xq = xnT_q[qp]
        if acc_guard is not None:
            ph.wait("sync", acc_guard[0], acc_guard[1])
        t_s = ph.dma("sync", x_acc[:].rearrange("p (i d) -> p i d", i=DC),
                     s_in[Q * 1024:(Q + 1) * 1024, :].rearrange("(i p) d -> p i d", p=128),
                     lda[1])
        if Q + 1 < NQ:
            load_xnq(Q + 1)
        ph.wait("tensor", ldxn[qp], xn_loads[Q])

        for tau in range(8):
            rp = tau % 2
            if psum_guard.get(("r", rp)) is not None:
                gs_, gt_ = psum_guard[("r", rp)]
                ph.wait("tensor", gs_, gt_)
            for c in range(DC):
                fn = lambda e, c=c, tau=tau, rp=rp, xq=xq: e.matmul(
                    ps_r[rp][:, 0:E0], xq[:, c * 1024 + tau * 128: c * 1024 + (tau + 1) * 128],
                    gwT_sb[:, c * E0:(c + 1) * E0], start=(c == 0), stop=(c == DC - 1))
                if c == DC - 1:
                    t_mm = ph.op("tensor", fn, sp)
                else:
                    ph.op("tensor", fn)
            ph.wait("scalar", sp, t_mm)
            t_sg = ph.op("scalar", lambda e, tau=tau, rp=rp: e.activation(
                scq[:, tau * E0:(tau + 1) * E0], ps_r[rp][:, 0:E0], AF.Sigmoid), sa)
            psum_guard[("r", rp)] = (sa, t_sg)

        ph.wait("vector", sa, t_sg)
        V = lambda fn: ph.op("vector", fn, sv)
        seg = lambda ap, n: ap.rearrange("p (t e) -> p t e", t=n)
        bc8 = lambda ap, n, w: ap.rearrange("p (t u) -> p t u", u=1).broadcast_to([128, n, w])
        V(lambda e: e.tensor_tensor(sbv8[:], scq[:], gbb8[:], ALU.add))
        pairs = [(0, 1), (0, 2), (0, 3), (1, 2), (1, 3), (2, 3)]
        for pi, (a, b) in enumerate(pairs):
            V(lambda e, a=a, b=b, pi=pi: e.tensor_tensor(
                prt8[:, pi * 32:(pi + 1) * 32], sbv8[:, a:RW:4], sbv8[:, b:RW:4], ALU.add))
        V(lambda e: e.tensor_tensor(gs8[:], prt8[:, 0:32], prt8[:, 32:64], ALU.max))
        for pi in range(2, 6):
            V(lambda e, pi=pi: e.tensor_tensor(gs8[:], gs8[:], prt8[:, pi * 32:(pi + 1) * 32], ALU.max))
        for i2, (a, b) in enumerate(pairs):
            dst = mn1 if i2 == 0 else mn2
            V(lambda e, a=a, b=b, dst=dst: e.tensor_tensor(
                dst[:], gs8[:, a:32:4], gs8[:, b:32:4], ALU.min))
            if i2 > 0:
                V(lambda e: e.tensor_tensor(mn1[:], mn1[:], mn2[:], ALU.max))
        V(lambda e: e.tensor_tensor(seg(gm8[:], NTQ), seg(gs8[:], NTQ), bc8(mn1[:], NTQ, 4), ALU.is_ge))
        for c in range(4):
            V(lambda e, c=c: e.tensor_copy(em8[:, c:RW:4], gm8[:]))
        V(lambda e: e.tensor_tensor(t18[:], sbv8[:], em8[:], ALU.mult))
        V(lambda e: e.tensor_scalar(msk8[:], em8[:], 1e30, -1e30, ALU.mult, ALU.add))
        V(lambda e: e.tensor_tensor(msk8[:], msk8[:], t18[:], ALU.add))
        for r in range(4):
            V(lambda e: e.tensor_reduce(mx8[:], seg(msk8[:], NTQ), axis=AX.X, op=ALU.max))
            V(lambda e: e.tensor_tensor(seg(sel8[:], NTQ), seg(msk8[:], NTQ), bc8(mx8[:], NTQ, E0), ALU.is_ge))
            if r == 0:
                V(lambda e: e.tensor_copy(wm8[:], sel8[:]))
            else:
                V(lambda e: e.tensor_tensor(wm8[:], wm8[:], sel8[:], ALU.add))
            if r < 3:
                V(lambda e: e.scalar_tensor_tensor(msk8[:], sel8[:], -2e30, msk8[:], ALU.mult, ALU.add))
        V(lambda e: e.tensor_tensor(t18[:], scq[:], wm8[:], ALU.mult))
        V(lambda e: e.tensor_reduce(rs8[:], seg(t18[:], NTQ), axis=AX.X, op=ALU.add))
        V(lambda e: e.tensor_scalar(rs8[:], rs8[:], 1e-20, None, ALU.add))
        V(lambda e: e.reciprocal(rc8[:], rs8[:]))
        V(lambda e: e.tensor_tensor(seg(comb[:], NTQ), seg(t18[:], NTQ), bc8(rc8[:], NTQ, E0), ALU.mult))
        t_cb = V(lambda e: e.tensor_scalar(comb[:], comb[:], 2.5, None, ALU.mult))

        for ei in range(NE):
            par = (Q * NE + ei) % 2
            ph.wait("tensor", ldw[par], wloads[Q * NE + ei])
            t_hact_last = None
            for s6 in range(IC):
                for tg in range(2):
                    gp = ps_g[(s6 * 2 + tg) % 2]
                    up = ps_u[(s6 * 2 + tg) % 2]
                    key = ("g", (s6 * 2 + tg) % 2)
                    if psum_guard.get(key) is not None:
                        gs_, gt_ = psum_guard[key]
                        ph.wait("tensor", gs_, gt_)
                    for c in range(DC):
                        fn = lambda e, c=c, s6=s6, tg=tg, gp=gp, par=par, xq=xq: e.matmul(
                            gp[:], wb1[par][:, c * I0 + s6 * 128: c * I0 + (s6 + 1) * 128],
                            xq[:, c * 1024 + tg * 512: c * 1024 + (tg + 1) * 512],
                            start=(c == 0), stop=(c == DC - 1))
                        if c == DC - 1:
                            t_gmm = ph.op("tensor", fn, sp)
                        else:
                            ph.op("tensor", fn)
                    key_u = ("u", (s6 * 2 + tg) % 2)
                    if psum_guard.get(key_u) is not None:
                        gs_, gt_ = psum_guard[key_u]
                        ph.wait("tensor", gs_, gt_)
                    for c in range(DC):
                        fn = lambda e, c=c, s6=s6, tg=tg, up=up, par=par, xq=xq: e.matmul(
                            up[:], wb3[par][:, c * I0 + s6 * 128: c * I0 + (s6 + 1) * 128],
                            xq[:, c * 1024 + tg * 512: c * 1024 + (tg + 1) * 512],
                            start=(c == 0), stop=(c == DC - 1))
                        if c == DC - 1:
                            t_umm = ph.op("tensor", fn, sp)
                        else:
                            ph.op("tensor", fn)
                    ph.wait("scalar", sp, t_gmm)
                    if s6 == 0 and tg == 0 and gsil_guard is not None:
                        ph.wait("scalar", sv, gsil_guard)
                    sl = gsil[:, s6 * 1024 + tg * 512: s6 * 1024 + (tg + 1) * 512]
                    hl = hact[:, s6 * 1024 + tg * 512: s6 * 1024 + (tg + 1) * 512]
                    t_sil = ph.op("scalar", lambda e, sl=sl, gp=gp: e.activation(
                        sl, gp[:], AF.Silu), sa)
                    psum_guard[key] = (sa, t_sil)
                    ph.wait("vector", sa, t_sil)
                    ph.wait("vector", sp, t_umm)
                    if s6 == 0 and tg == 0 and hact_guard is not None:
                        ph.wait("vector", sp, hact_guard)
                    t_h = ph.op("vector", lambda e, hl=hl, sl=sl, up=up: e.tensor_tensor(
                        hl, sl, up[:], ALU.mult), sv)
                    psum_guard[key_u] = (sv, t_h)
            gsil_guard = sv.v
            t_hact_done = sv.v
            ph.wait("tensor", sv, t_hact_done)
            ph.wait("vector", lda[1], t_s)
            if Q > 0 and psum_guard.get("acc_ld") is not None:
                pass
            for tau in range(8):
                for n in range(2):
                    yp = ps_y[n]
                    key = ("y", n)
                    if psum_guard.get(key) is not None:
                        gs_, gt_ = psum_guard[key]
                        ph.wait("tensor", gs_, gt_)
                    for s6 in range(IC):
                        fn = lambda e, s6=s6, tau=tau, n=n, yp=yp, par=par: e.matmul(
                            yp[:], hact[:, s6 * 1024 + tau * 128: s6 * 1024 + (tau + 1) * 128],
                            wb2[par][:, s6 * D + n * 512: s6 * D + (n + 1) * 512],
                            start=(s6 == 0), stop=(s6 == IC - 1))
                        if s6 == IC - 1:
                            t_ymm = ph.op("tensor", fn, sp)
                        else:
                            ph.op("tensor", fn)
                    ph.wait("vector", sp, t_ymm)
                    xsl = x_acc[:, tau * 1024 + n * 512: tau * 1024 + (n + 1) * 512]
                    ccol = comb[:, tau * E0 + ei: tau * E0 + ei + 1] if ei < E0 else onescol[:]
                    t_y = ph.op("vector", lambda e, xsl=xsl, yp=yp, ccol=ccol: e.scalar_tensor_tensor(
                        xsl, yp[:], ccol, xsl, ALU.mult, ALU.add), sv)
                    psum_guard[key] = (sv, t_y)
            hact_guard = sp.v
            wload_guard[par] = sp.v
            if nextload < NQ * NE:
                wloads[nextload] = load_we(nextload % NE, nextload % 2)
                nextload += 1
        t_acc_done = sv.v
        ph.wait("sync", sv, t_acc_done)
        t_st = ph.dma("sync", x_out[Q * 1024:(Q + 1) * 1024, :].rearrange("(i p) d -> p i d", p=128),
                      x_acc[:].rearrange("p (i d) -> p i d", i=DC), sox)
        acc_guard = (sox, t_st)
        xnq_guard[qp] = sp.v

    ph.wait("sync", sox, sox.v)



def emit_layer_f1(ph, xnT_in, s_in, r1T, w1T, w2T, x_out):
    nc = ph.nc
    NE8 = T // 512
    HC = H1 // 128
    xnT_e = [ph.sb(f"f1xnT{i}", [128, DC * 512], BF16) for i in range(2)]
    x_acc = ph.sb("f1xacc", [128, 4 * D], F32)
    wb2 = ph.sb("f1w2", [128, HC * D], BF16)
    wb1 = [ph.sb(f"f1w1_{i}", [128, DC * 1024], BF16) for i in range(2)]
    hact = ph.sb("f1hact", [128, HC * 512], BF16)
    r1T_sb = ph.sb("f1r1", [128, DC * E1], BF16)
    RW1 = 4 * E1
    lgt = ph.sb("f1lgt", [128, RW1], F32)
    exp8 = ph.sb("f1exp", [128, RW1], F32)
    pb8 = ph.sb("f1pb", [128, RW1], F32)
    sel8 = ph.sb("f1sel", [128, RW1], F32)
    msk8 = ph.sb("f1msk", [128, RW1], F32)
    wm8 = ph.sb("f1wm", [128, RW1], F32)
    wgt = ph.sb("f1wgt", [128, RW1], F32)
    nm4 = ph.sb("f1nm", [128, 4], F32)
    rs4 = ph.sb("f1rs", [128, 4], F32)
    rc4 = ph.sb("f1rc", [128, 4], F32)
    mx4 = ph.sb("f1mx", [128, 4], F32)

    ps_r = [ph.ps(f"f1psr{i}", [128, 512], F32) for i in range(2)]
    ps_h = [ph.ps(f"f1psh{i}", [128, 512], F32) for i in range(3)]
    ps_y = [ph.ps(f"f1psy{i}", [128, 512], F32) for i in range(2)]

    lda = [ph.sem("lda0"), ph.sem("lda1")]
    ldw1 = [ph.sem("ldw1_0"), ph.sem("ldw1_1")]
    ldw2 = ph.sem("ldw2")
    ldc = ph.sem("ldc")
    sox = ph.sem("sox")
    sp = ph.sem("sp")
    sv = ph.esem("vector")
    sa = ph.esem("scalar")

    ph.dma("sync", r1T_sb[:].rearrange("p (c e) -> p c e", c=DC),
           r1T.rearrange("(c p) e -> p c e", p=128), ldc)

    wload_guard = {}
    w2_guard = None
    psum_guard = {}
    acc_guard = None
    xne_guard = {}
    hact_guard = None
    wloads = {}
    w2loads = {}
    ldxn = [ph.sem("ldxn0"), ph.sem("ldxn1")]
    xn_loads = {}

    def load_w1(k, par):
        ei = (k // 4) % E1
        c = k % 4
        if wload_guard.get(par) is not None:
            ph.wait("gpsimd", sp, wload_guard[par])
        return ph.dma("gpsimd", wb1[par][:].rearrange("p (c2 h) -> p c2 h", c2=DC),
                      w1T[ei][:, c * 1024:(c + 1) * 1024].rearrange("(c2 p) h -> p c2 h", p=128),
                      ldw1[par])

    def load_w2(ke):
        if w2_guard is not None:
            ph.wait("scalar", sp, w2_guard) if False else None
            ph.wait("gpsimd", sp, w2_guard)
        return ph.dma("gpsimd", wb2[:].rearrange("p (s d) -> p s d", s=HC),
                      w2T[ke % E1].rearrange("(s p) d -> p s d", p=128), ldw2)

    wloads[0] = load_w1(0, 0)
    wloads[1] = load_w1(1, 1)
    w2loads[0] = load_w2(0)
    next1 = 2
    next2 = 1

    def load_xne(Q):
        qp = Q % 2
        g = xne_guard.get(qp)
        if g is not None:
            ph.wait("sync", sp, g)
        xn_loads[Q] = ph.dma("sync", xnT_e[qp][:].rearrange("p (c t) -> p c t", c=DC),
                             xnT_in.rearrange("(c p) t -> p c t", p=128)[:, :, Q * 512:(Q + 1) * 512],
                             ldxn[qp])

    load_xne(0)
    for Q in range(NE8):
        qp = Q % 2
        xq = xnT_e[qp]
        if acc_guard is not None:
            ph.wait("sync", acc_guard[0], acc_guard[1])
        t_s = ph.dma("sync", x_acc[:].rearrange("p (i d) -> p i d", i=4),
                     s_in[Q * 512:(Q + 1) * 512, :].rearrange("(i p) d -> p i d", p=128),
                     lda[1])
        if Q + 1 < NE8:
            load_xne(Q + 1)
        ph.wait("tensor", ldxn[qp], xn_loads[Q])
        if Q == 0:
            ph.wait("tensor", ldc, ldc.v)

        for tau in range(4):
            rp = tau % 2
            if psum_guard.get(("r", rp)) is not None:
                gs_, gt_ = psum_guard[("r", rp)]
                ph.wait("tensor", gs_, gt_)
            for c in range(DC):
                fn = lambda e, c=c, tau=tau, rp=rp, xq=xq: e.matmul(
                    ps_r[rp][:, 0:E1], xq[:, c * 512 + tau * 128: c * 512 + (tau + 1) * 128],
                    r1T_sb[:, c * E1:(c + 1) * E1], start=(c == 0), stop=(c == DC - 1))
                if c == DC - 1:
                    t_mm = ph.op("tensor", fn, sp)
                else:
                    ph.op("tensor", fn)
            ph.wait("scalar", sp, t_mm)
            lg_t = lgt[:, tau * E1:(tau + 1) * E1]
            t_lg = ph.op("scalar", lambda e, lg_t=lg_t, rp=rp: e.activation(lg_t, ps_r[rp][:, 0:E1], AF.Copy), sa)
            psum_guard[("r", rp)] = (sa, t_lg)

        ph.wait("vector", sa, t_lg)
        V = lambda fn: ph.op("vector", fn, sv)
        seg = lambda ap: ap.rearrange("p (t e) -> p t e", t=4)
        bc4 = lambda ap, w: ap.rearrange("p (t u) -> p t u", u=1).broadcast_to([128, 4, w])
        t_nm = V(lambda e: e.tensor_reduce(nm4[:], seg(lgt[:]), axis=AX.X, op=ALU.max, negate=True))
        V(lambda e: e.tensor_tensor(seg(exp8[:]), seg(lgt[:]), bc4(nm4[:], E1), ALU.add))
        ph.wait("scalar", sv, sv.v)
        t_ex = ph.op("scalar", lambda e: e.activation(exp8[:], exp8[:], AF.Exp), sa)
        ph.wait("vector", sa, t_ex)
        V(lambda e: e.tensor_reduce(rs4[:], seg(exp8[:]), axis=AX.X, op=ALU.add))
        V(lambda e: e.reciprocal(rc4[:], rs4[:]))
        V(lambda e: e.tensor_tensor(seg(pb8[:]), seg(exp8[:]), bc4(rc4[:], E1), ALU.mult))
        V(lambda e: e.tensor_reduce(mx4[:], seg(pb8[:]), axis=AX.X, op=ALU.max))
        V(lambda e: e.tensor_tensor(seg(sel8[:]), seg(pb8[:]), bc4(mx4[:], E1), ALU.is_ge))
        V(lambda e: e.scalar_tensor_tensor(msk8[:], sel8[:], -2.0, pb8[:], ALU.mult, ALU.add))
        V(lambda e: e.tensor_reduce(mx4[:], seg(msk8[:]), axis=AX.X, op=ALU.max))
        V(lambda e: e.tensor_tensor(seg(msk8[:]), seg(msk8[:]), bc4(mx4[:], E1), ALU.is_ge))
        V(lambda e: e.tensor_tensor(wm8[:], sel8[:], msk8[:], ALU.add))
        V(lambda e: e.tensor_tensor(wgt[:], pb8[:], wm8[:], ALU.mult))

        for ei in range(E1):
            ke = Q * E1 + ei
            for c in range(4):
                kk = ke * 4 + c
                par = kk % 2
                ph.wait("tensor", ldw1[par], wloads[kk])
                for s8 in range(DC):
                    hp = ps_h[(c * DC + s8) % 3]
                    key = ("h", (c * DC + s8) % 3)
                    if psum_guard.get(key) is not None:
                        gs_, gt_ = psum_guard[key]
                        ph.wait("tensor", gs_, gt_)
                    for dc in range(DC):
                        fn = lambda e, dc=dc, s8=s8, hp=hp, par=par, xq=xq: e.matmul(
                            hp[:], wb1[par][:, dc * 1024 + s8 * 128: dc * 1024 + (s8 + 1) * 128],
                            xq[:, dc * 512:(dc + 1) * 512],
                            start=(dc == 0), stop=(dc == DC - 1))
                        if dc == DC - 1:
                            t_hm = ph.op("tensor", fn, sp)
                        else:
                            ph.op("tensor", fn)
                    ph.wait("scalar", sp, t_hm)
                    if c == 0 and s8 == 0 and hact_guard is not None:
                        ph.wait("scalar", sp, hact_guard)
                    hl = hact[:, (c * DC + s8) * 512: (c * DC + s8 + 1) * 512]
                    t_g = ph.op("scalar", lambda e, hl=hl, hp=hp: e.activation(
                        hl, hp[:], AF.Gelu), sa)
                    psum_guard[key] = (sa, t_g)
                wload_guard[par] = sp.v
                if next1 < NE8 * E1 * 4:
                    wloads[next1] = load_w1(next1, next1 % 2)
                    next1 += 1
            t_h_done = sa.v
            ph.wait("tensor", sa, t_h_done)
            ph.wait("tensor", ldw2, w2loads[ke])
            ph.wait("vector", lda[1], t_s)
            for tau in range(4):
                for n in range(2):
                    yp = ps_y[n]
                    key = ("y", n)
                    if psum_guard.get(key) is not None:
                        gs_, gt_ = psum_guard[key]
                        ph.wait("tensor", gs_, gt_)
                    for s in range(HC):
                        fn = lambda e, s=s, tau=tau, n=n, yp=yp: e.matmul(
                            yp[:], hact[:, s * 512 + tau * 128: s * 512 + (tau + 1) * 128],
                            wb2[:, s * D + n * 512: s * D + (n + 1) * 512],
                            start=(s == 0), stop=(s == HC - 1))
                        if s == HC - 1:
                            t_ym = ph.op("tensor", fn, sp)
                        else:
                            ph.op("tensor", fn)
                    ph.wait("vector", sp, t_ym)
                    xsl = x_acc[:, tau * D + n * 512: tau * D + (n + 1) * 512]
                    wcol = wgt[:, tau * E1 + ei: tau * E1 + ei + 1]
                    t_y = ph.op("vector", lambda e, xsl=xsl, yp=yp, wcol=wcol: e.scalar_tensor_tensor(
                        xsl, yp[:], wcol, xsl, ALU.mult, ALU.add), sv)
                    psum_guard[key] = (sv, t_y)
            hact_guard = sp.v
            w2_guard = sp.v
            if next2 < NE8 * E1:
                w2loads[next2] = load_w2(next2)
                next2 += 1
        t_acc_done = sv.v
        ph.wait("sync", sv, t_acc_done)
        t_st = ph.dma("sync", x_out[Q * 512:(Q + 1) * 512, :].rearrange("(i p) d -> p i d", p=128),
                      x_acc[:].rearrange("p (i d) -> p i d", i=4), sox)
        acc_guard = (sox, t_st)
        xne_guard[qp] = sp.v

    ph.wait("sync", sox, sox.v)


def build_layer_f_program(layer):
    nc = bass.Bass("TRN2", target_bir_lowering=False, debug=False)
    xnT = nc.dram_tensor("xnT", [D, T], BF16, kind="ExternalInput").ap()
    s_in = nc.dram_tensor("s", [T, D], F32, kind="ExternalInput").ap()
    x_out = nc.dram_tensor("xout", [T, D], F32, kind="ExternalOutput").ap()
    with ExitStack() as es:
        ph = Phase(nc, es, f"f{layer}_")
        if layer == 0:
            gwT = nc.dram_tensor("gwT", [D, E0], BF16, kind="ExternalInput").ap()
            gb = nc.dram_tensor("gb", [E0], F32, kind="ExternalInput").ap()
            w1T = nc.dram_tensor("w1T0", [E0, D, I0], BF16, kind="ExternalInput").ap()
            w3T = nc.dram_tensor("w3T0", [E0, D, I0], BF16, kind="ExternalInput").ap()
            w2T = nc.dram_tensor("w2T0", [E0, I0, D], BF16, kind="ExternalInput").ap()
            ws1T = nc.dram_tensor("ws1T", [D, I0], BF16, kind="ExternalInput").ap()
            ws3T = nc.dram_tensor("ws3T", [D, I0], BF16, kind="ExternalInput").ap()
            ws2T = nc.dram_tensor("ws2T", [I0, D], BF16, kind="ExternalInput").ap()
            emit_layer_f0(ph, xnT, s_in, gwT, gb, w1T, w3T, w2T, ws1T, ws3T, ws2T, x_out)
        else:
            r1T = nc.dram_tensor("r1T", [D, E1], BF16, kind="ExternalInput").ap()
            w1T = nc.dram_tensor("w1T1", [E1, D, H1], BF16, kind="ExternalInput").ap()
            w2T = nc.dram_tensor("w2T1", [E1, H1, D], BF16, kind="ExternalInput").ap()
            emit_layer_f1(ph, xnT, s_in, r1T, w1T, w2T, x_out)
        ph.emit()
    return nc



_PROG_CACHE = {}


def _prog(key, builder):
    if key not in _PROG_CACHE:
        _PROG_CACHE[key] = builder()
    return _PROG_CACHE[key]


def kernel(**inputs):
    return kernel_fused(**inputs)


def kernel_multilaunch(**inputs):
    x = np.asarray(inputs["L0_attn_x"])
    assert x.shape == (NCORES, T, D)
    shards = [np.ascontiguousarray(x[i]) for i in range(NCORES)]
    cores = list(range(NCORES))

    prep_ins = {n: np.ascontiguousarray(np.asarray(inputs[n])) for n, _ in PREP_INS}
    nc_p = _prog("prep", build_prep_program)
    pw = run_bass_kernel_spmd(nc_p, [prep_ins] * NCORES, core_ids=cores).results

    nc_a0 = _prog("a0", lambda: build_layer_attn_program(0))
    ins = [dict(x=shards[i], wq0b=pw[i]["wq0b"], wk0b=pw[i]["wk0b"], wv0b=pw[i]["wv0b"])
           for i in range(NCORES)]
    ra0 = run_bass_kernel_spmd(nc_a0, ins, core_ids=cores).results

    nc_f0 = _prog("f0", lambda: build_layer_f_program(0))
    gb = np.ascontiguousarray(np.asarray(inputs["L0_ffn_gate_bias"]))
    ins = [dict(xnT=ra0[i]["xnT"], s=ra0[i]["s"], gwT=pw[i]["gwT"], gb=gb,
                w1T0=pw[i]["w1T0"], w3T0=pw[i]["w3T0"], w2T0=pw[i]["w2T0"],
                ws1T=pw[i]["ws1T"], ws3T=pw[i]["ws3T"], ws2T=pw[i]["ws2T"])
           for i in range(NCORES)]
    rf0 = run_bass_kernel_spmd(nc_f0, ins, core_ids=cores).results

    nc_a1 = _prog("a1", lambda: build_layer_attn_program(1))
    bq = np.ascontiguousarray(np.asarray(inputs["L1_attn_bq"]))
    bk = np.ascontiguousarray(np.asarray(inputs["L1_attn_bk"]))
    bv = np.ascontiguousarray(np.asarray(inputs["L1_attn_bv"]))
    ins = [dict(x=rf0[i]["xout"], q1T=pw[i]["q1T"], k1T=pw[i]["k1T"], v1T=pw[i]["v1T"],
                bq=bq, bk=bk, bv=bv) for i in range(NCORES)]
    ra1 = run_bass_kernel_spmd(nc_a1, ins, core_ids=cores).results

    nc_f1 = _prog("f1", lambda: build_layer_f_program(1))
    ins = [dict(xnT=ra1[i]["xnT"], s=ra1[i]["s"], r1T=pw[i]["r1T"],
                w1T1=pw[i]["w1T1"], w2T1=pw[i]["w2T1"]) for i in range(NCORES)]
    rf1 = run_bass_kernel_spmd(nc_f1, ins, core_ids=cores).results

    return np.stack([rf1[i]["xout"] for i in range(NCORES)]).astype(np.float32)



def build_full_program():
    nc = bass.Bass("TRN2", target_bir_lowering=False, debug=False)
    x_in = nc.dram_tensor("x", [T, D], F32, kind="ExternalInput").ap()
    src = {n: nc.dram_tensor(n, s, F32, kind="ExternalInput").ap() for n, s in PREP_INS}
    gb = nc.dram_tensor("L0_ffn_gate_bias", [E0], F32, kind="ExternalInput").ap()
    bq = nc.dram_tensor("L1_attn_bq", [D], F32, kind="ExternalInput").ap()
    bk = nc.dram_tensor("L1_attn_bk", [D], F32, kind="ExternalInput").ap()
    bv = nc.dram_tensor("L1_attn_bv", [D], F32, kind="ExternalInput").ap()
    out = nc.dram_tensor("out", [T, D], F32, kind="ExternalOutput").ap()

    pw = {n: nc.dram_tensor("pw_" + n, s, BF16, kind="Internal").ap() for n, s in PREP_OUTS}
    xnT0 = nc.dram_tensor("xnT0", [D, T], BF16, kind="Internal").ap()
    qT0 = nc.dram_tensor("qT0", [D, T], BF16, kind="Internal").ap()
    s0 = nc.dram_tensor("s0", [T, D], F32, kind="Internal").ap()
    x1 = nc.dram_tensor("x1", [T, D], F32, kind="Internal").ap()
    xnT1 = nc.dram_tensor("xnT1", [D, T], BF16, kind="Internal").ap()
    qT1 = nc.dram_tensor("qT1", [D, T], BF16, kind="Internal").ap()
    s1 = nc.dram_tensor("s1", [T, D], F32, kind="Internal").ap()

    cj, tj0, tj1, sj = prep_job_lists(src, pw)
    stage_of = {}
    bgl = []
    for (s_ap, d_ap) in tj0 + tj1:
        nmb = "st_" + d_ap.tensor.name if hasattr(d_ap, "tensor") else None
        h = nc.dram_tensor(f"bstg{len(bgl)}", list(s_ap.shape), BF16, kind="Internal").ap()
        stage_of[id(s_ap)] = h
        bgl.append((h, s_ap))
    tj0b = [(stage_of[id(s_ap)], d_ap) for (s_ap, d_ap) in tj0]
    tj1b = [(stage_of[id(s_ap)], d_ap) for (s_ap, d_ap) in tj1]
    with ExitStack() as es:
        with ExitStack() as esp:
            php = Phase(nc, esp, "p_", sem_es=es)
            emit_prep(php, [], cj, sj)
            php.emit()
        with ExitStack() as esl:
            kT_sb = esl.enter_context(nc.sbuf_tensor("kT0", [128, DC * T], BF16))
            v_sb = esl.enter_context(nc.sbuf_tensor("v0", [128, NT * D], BF16))
            nh = len(bgl) // 3
            with ExitStack() as e1:
                ph = Phase(nc, e1, "a10_", sem_es=es)
                emit_layer_a1(ph, x_in, pw["wq0b"], pw["wk0b"], pw["wv0b"],
                              xnT0, qT0, kT_sb, v_sb, None, bg_dmas=bgl[:nh])
                ph.emit()
            with ExitStack() as e2:
                ph = Phase(nc, e2, "a20_", sem_es=es)
                emit_layer_a2(ph, "retnet", x_in, qT0, kT_sb, v_sb, s0, bg_dmas=bgl[nh:])
                ph.emit()
        with ExitStack() as esp2:
            php2 = Phase(nc, esp2, "p2_", sem_es=es)
            emit_prep(php2, tj0b + tj1b, [], [], src_bf16=True)
            php2.emit()
        with ExitStack() as e3:
            ph = Phase(nc, e3, "f0_", sem_es=es)
            emit_layer_f0(ph, xnT0, s0, pw["gwT"], gb, pw["w1T0"], pw["w3T0"], pw["w2T0"],
                          pw["ws1T"], pw["ws3T"], pw["ws2T"], x1)
            ph.emit()
        with ExitStack() as esl:
            kT_sb = esl.enter_context(nc.sbuf_tensor("kT1", [128, DC * T], BF16))
            v_sb = esl.enter_context(nc.sbuf_tensor("v1s", [128, NT * D], BF16))
            with ExitStack() as e4:
                ph = Phase(nc, e4, "a11_", sem_es=es)
                emit_layer_a1(ph, x1, pw["q1T"], pw["k1T"], pw["v1T"],
                              xnT1, qT1, kT_sb, v_sb, (bq, bk, bv))
                ph.emit()
            with ExitStack() as e5:
                ph = Phase(nc, e5, "a21_", sem_es=es)
                emit_layer_a2(ph, "softmax", x1, qT1, kT_sb, v_sb, s1)
                ph.emit()
        with ExitStack() as e6:
            ph = Phase(nc, e6, "f1_", sem_es=es)
            emit_layer_f1(ph, xnT1, s1, pw["r1T"], pw["w1T1"], pw["w2T1"], out)
            ph.emit()
    return nc


def kernel_fused(**inputs):
    x = np.asarray(inputs["L0_attn_x"])
    shards = [np.ascontiguousarray(x[i]) for i in range(NCORES)]
    base = {n: np.ascontiguousarray(np.asarray(inputs[n])) for n, _ in PREP_INS}
    for n in ("L0_ffn_gate_bias", "L1_attn_bq", "L1_attn_bk", "L1_attn_bv"):
        base[n] = np.ascontiguousarray(np.asarray(inputs[n]))
    nc = _prog("full", build_full_program)
    in_maps = [dict(base, x=shards[i]) for i in range(NCORES)]
    res = run_bass_kernel_spmd(nc, in_maps, core_ids=list(range(NCORES)))
    return np.stack([res.results[i]["out"] for i in range(NCORES)]).astype(np.float32)
